# version 08
# baseline (speedup 1.0000x reference)
import numpy as np
import sys

sys.path.insert(0, "/opt/trn_rl_repo")
import concourse.bass as bass
import concourse.mybir as mybir
from concourse.bass_utils import run_bass_kernel_spmd
from contextlib import ExitStack
import ml_dtypes

F32 = mybir.dt.float32
BF16 = mybir.dt.bfloat16
AF = mybir.ActivationFunctionType
ALU = mybir.AluOpType
AX = mybir.AxisListType

NCORES = 8
DBG_MAMBA = False
D = 2048
NDC = 16
TPC = 2048
BLK = 1024
EPS = 1e-5


def owner(j):
    return j if j < 4 else 7 - j


def slot(j):
    return 0 if j < 4 else 1


class Buf:
    __slots__ = ("w", "r")

    def __init__(self):
        self.w = None
        self.r = []


class Prog:
    ENGS = ("sp", "pe", "act", "dve", "pool", "sp2")

    def __init__(self, nc):
        self.nc = nc
        self.q = {e: [] for e in self.ENGS}
        self.cnt = {e: 0 for e in self.ENGS}
        self.auto = {"act", "dve", "pool"}
        self.semh = {}

    def raw_op(self, eng, fn, waits=(), dma=False, inc=None):
        amt = 16 if dma else (1 if inc is None else inc)
        red = {}
        for (we, wc) in waits:
            if wc and (we not in red or red[we] < wc):
                red[we] = wc
        if eng in self.auto and self.cnt[eng] > 0:
            red[eng] = max(red.get(eng, 0), self.cnt[eng])
        if eng == "pe":
            red.pop("pe", None)
        w = tuple((we, wc) for we, wc in red.items())
        self.cnt[eng] += amt
        self.q[eng].append((fn, w, amt))
        return (eng, self.cnt[eng])

    def op(self, eng, fn, ins=(), outs=(), waits=(), dma=False, inc=None):
        wl = list(waits)
        for b in ins:
            if b.w:
                wl.append(b.w)
        for b in outs:
            if b.w:
                wl.append(b.w)
            wl.extend(b.r)
        t = self.raw_op(eng, fn, waits=wl, dma=dma, inc=inc)
        for b in ins:
            b.r.append(t)
        for b in outs:
            b.w = t
            b.r = []
        return t

    def tok(self, eng):
        return (eng, self.cnt[eng])

    def barrier(self):
        snap = {e: self.cnt[e] for e in self.ENGS}
        for e in self.ENGS:
            if e == "sp2":
                continue
            waits = [(e2, c) for e2, c in snap.items() if c]
            self.raw_op(e, lambda r: r.nop(), waits=waits)

    def build(self, block, ctx):
        for e in self.ENGS:
            self.semh[e] = ctx.enter_context(self.nc.semaphore(f"sem_{e}"))
        for e in self.ENGS:
            queue = self.q[e]
            if not queue:
                continue
            semh = self.semh
            myself = self.semh[e]

            def replay(raw, queue=queue, myself=myself):
                for fn, waits, amt in queue:
                    for (we, wc) in waits:
                        raw.wait_ge(semh[we], wc)
                    inst = fn(raw)
                    inst.then_inc(myself, amt)

            if e == "sp":
                block.sync(replay)
            elif e == "sp2":
                raise AssertionError("sp2 queue unused (would collide with act)")
            elif e == "pe":
                block.tensor(replay)
            elif e == "act":
                block.scalar(replay)
            elif e == "dve":
                block.vector(replay)
            elif e == "pool":
                block.gpsimd(replay)


class Common:
    def __init__(self, P, nc, ctx):
        e = ctx.enter_context
        self.ones_col_bf = e(nc.sbuf_tensor("ones_col_bf", [128, 1], BF16))
        self.ones_row_f = e(nc.sbuf_tensor("ones_row_f", [1, 128], F32))
        self.ones65 = e(nc.sbuf_tensor("ones65", [65, 128], F32))
        self.ones_row_bf = e(nc.sbuf_tensor("ones_row_bf", [1, 128], BF16))
        self.ident_bf = e(nc.sbuf_tensor("ident_bf", [128, 128], BF16))
        self.ident_f = e(nc.sbuf_tensor("ident_f", [128, 128], F32))
        scr = self.ident_f
        P.raw_op("pool", lambda g: g.memset(self.ones_col_bf[:], 1.0))
        P.raw_op("pool", lambda g: g.memset(self.ones_row_f[:], 1.0))
        P.raw_op("pool", lambda g: g.memset(self.ones65[:], 1.0))
        P.raw_op("pool", lambda g: g.memset(self.ones_row_bf[:], 1.0))
        P.raw_op("pool", lambda g: g.iota(
            scr[:], pattern=[[1, 128]], base=0, channel_multiplier=-1,
            allow_small_or_imprecise_dtypes=True))
        t1 = P.raw_op("dve", lambda v: v.tensor_scalar(
            out=self.ident_bf[:], in0=scr[:], scalar1=0.0, scalar2=None,
            op0=ALU.is_equal), waits=[P.tok("pool")])
        t2 = P.raw_op("dve", lambda v: v.tensor_scalar(
            out=self.ident_f[:], in0=scr[:], scalar1=0.0, scalar2=None,
            op0=ALU.is_equal))
        self.ready = [P.tok("pool"), t1, t2]


class LnScratch:
    def __init__(self, nc, ctx, name):
        e = ctx.enter_context
        x0 = e(nc.sbuf_tensor(f"{name}_xsq0", [128, 512], BF16))
        self.xsq = [x0, x0]
        self._stats = e(nc.sbuf_tensor(f"{name}_st", [1, 3 * 512], F32))
        self.stat = [self._stats[0:1, i * 512:(i + 1) * 512] for i in range(3)]
        self.rstd_bc = e(nc.sbuf_tensor(f"{name}_rstd", [128, 512], BF16))
        self.mr_bc = e(nc.sbuf_tensor(f"{name}_mr", [128, 512], BF16))
        self.psA = e(nc.psum_tensor(f"{name}_psA", [1, 512], F32))
        self.psB = e(nc.psum_tensor(f"{name}_psB", [1, 512], F32))
        self.psbc = e(nc.psum_tensor(f"{name}_psbc", [128, 512], F32))
        self.b = {k: Buf() for k in
                  ("xsq0", "st0", "st1", "st2",
                   "rstd", "mr", "psA", "psB", "psbc")}
        self.b["xsq1"] = self.b["xsq0"]


def emit_ln(P, cm, X, bX, T, xln_out, bXout, ls,
            gamma_sb=None, beta_sb=None, bparams=None):
    b = ls.b
    pw = [bparams] if bparams else []
    for tc in range(T // 512):
        t0 = tc * 512
        for dc in range(NDC):
            P.op("pe", lambda pe, dc=dc, t0=t0: pe.matmul(
                ls.psA[:], cm.ones_col_bf[:],
                X[:, dc * T + t0: dc * T + t0 + 512],
                start=(dc == 0), stop=(dc == NDC - 1)),
                ins=[bX], outs=[b["psA"]], waits=cm.ready)
        for dc in range(NDC):
            sb = dc % 2
            P.op("act", lambda s, dc=dc, t0=t0, sb=sb: s.activation(
                out=ls.xsq[sb][:], in_=X[:, dc * T + t0: dc * T + t0 + 512],
                func=AF.Square), ins=[bX], outs=[b[f"xsq{sb}"]])
            P.op("pe", lambda pe, dc=dc, sb=sb: pe.matmul(
                ls.psB[:], cm.ones_col_bf[:], ls.xsq[sb][:],
                start=(dc == 0), stop=(dc == NDC - 1)),
                ins=[b[f"xsq{sb}"]], outs=[b["psB"]])
        P.op("dve", lambda v: v.tensor_scalar(
            out=ls.stat[0][:], in0=ls.psA[:], scalar1=1.0 / D, scalar2=None,
            op0=ALU.mult), ins=[b["psA"]], outs=[b["st0"]])
        P.op("dve", lambda v: v.scalar_tensor_tensor(
            out=ls.stat[1][:], in0=ls.stat[0][:], scalar=-1.0, in1=ls.stat[0][:],
            op0=ALU.mult, op1=ALU.mult), ins=[b["st0"]], outs=[b["st1"]])
        P.op("dve", lambda v: v.scalar_tensor_tensor(
            out=ls.stat[1][:], in0=ls.psB[:], scalar=1.0 / D, in1=ls.stat[1][:],
            op0=ALU.mult, op1=ALU.add), ins=[b["psB"], b["st1"]], outs=[b["st1"]])
        P.op("dve", lambda v: v.tensor_scalar(
            out=ls.stat[1][:], in0=ls.stat[1][:], scalar1=EPS, scalar2=None,
            op0=ALU.add), ins=[b["st1"]], outs=[b["st1"]])
        P.op("act", lambda s: s.activation(
            out=ls.stat[2][:], in_=ls.stat[1][:], func=AF.Sqrt),
            ins=[b["st1"]], outs=[b["st2"]])
        P.op("dve", lambda v: v.reciprocal(ls.stat[1][:], ls.stat[2][:]),
             ins=[b["st2"]], outs=[b["st1"]])
        P.op("dve", lambda v: v.tensor_mul(
            ls.stat[2][:], ls.stat[0][:], ls.stat[1][:]),
            ins=[b["st0"], b["st1"]], outs=[b["st2"]])
        P.op("pe", lambda pe: pe.matmul(
            ls.psbc[:], cm.ones_row_f[:], ls.stat[1][:], start=True, stop=True),
            ins=[b["st1"]], outs=[b["psbc"]])
        P.op("dve", lambda v: v.tensor_copy(ls.rstd_bc[:], ls.psbc[:]),
             ins=[b["psbc"]], outs=[b["rstd"]])
        P.op("pe", lambda pe: pe.matmul(
            ls.psbc[:], cm.ones_row_f[:], ls.stat[2][:], start=True, stop=True),
            ins=[b["st2"]], outs=[b["psbc"]])
        P.op("dve", lambda v: v.tensor_copy(ls.mr_bc[:], ls.psbc[:]),
             ins=[b["psbc"]], outs=[b["mr"]])
        for dc in range(NDC):
            xs = X[:, dc * T + t0: dc * T + t0 + 512]
            xo = xln_out[:, dc * T + t0: dc * T + t0 + 512]
            P.op("dve", lambda v, xs=xs, xo=xo: v.tensor_mul(
                xo, xs, ls.rstd_bc[:]),
                ins=[bX, b["rstd"]], outs=[bXout] if bXout is not bX else [bX])
            P.op("dve", lambda v, xo=xo: v.tensor_sub(xo, xo, ls.mr_bc[:]),
                 ins=[b["mr"]], outs=[bXout])
            if gamma_sb is not None:
                P.op("dve", lambda v, xo=xo, dc=dc: v.tensor_scalar(
                    out=xo, in0=xo, scalar1=gamma_sb[:, dc:dc + 1],
                    scalar2=beta_sb[:, dc:dc + 1], op0=ALU.mult, op1=ALU.add),
                    ins=pw, outs=[bXout])
    return bXout


def load_stream(P, nc, X, bX, x_in):
    return P.op("pool", lambda g: g.dma_start(
        out=X[:], in_=x_in.rearrange("(a p) t -> p a t", p=128)),
        outs=[bX], dma=True)


L1FFN_INPUTS = {"wT": [D, D], "gamma": [128, NDC], "beta": [128, NDC],
                "lnw": [128, NDC], "lnb": [128, NDC]}


def emit_l1ffn(P, nc, cm, A, pfx="lf"):
    x_in, x_out = A["x_in"], A["x_out"]
    wT, gamma, beta, lnw, lnb = A["wT"], A["gamma"], A["beta"], A["lnw"], A["lnb"]
    with ExitStack() as ctx:
        e = ctx.enter_context
        U = e(nc.sbuf_tensor(pfx + "U", [128, NDC * TPC], BF16))
        X = U
        wsg = U
        xn = e(nc.sbuf_tensor(pfx + "xn", [128, NDC * TPC], BF16))
        gam = e(nc.sbuf_tensor(pfx + "gam", [128, NDC], F32))
        bet = e(nc.sbuf_tensor(pfx + "bet", [128, NDC], F32))
        lw = e(nc.sbuf_tensor(pfx + "lw", [128, NDC], F32))
        lb = e(nc.sbuf_tensor(pfx + "lb", [128, NDC], F32))
        wf = [e(nc.sbuf_tensor(pfx + f"wf{i}", [128, 1024], F32)) for i in range(2)]
        absm = e(nc.sbuf_tensor(pfx + "absm", [128, 2], F32))
        maxrow = e(nc.sbuf_tensor(pfx + "maxrow", [1, 1], F32))
        sg = e(nc.sbuf_tensor(pfx + "sg", [128, NDC], F32))
        stage = [e(nc.sbuf_tensor(pfx + f"stage{i}", [128, 512], F32)) for i in range(2)]
        resd = [e(nc.sbuf_tensor(pfx + f"resd{i}", [128, 512], F32)) for i in range(2)]
        pstr = e(nc.psum_tensor(pfx + "pstr", [1, 128], F32))
        psout = [e(nc.psum_tensor(pfx + f"psout{i}", [128, 512], F32)) for i in range(2)]
        psb1 = e(nc.psum_tensor(pfx + "psb1", [128, 1], F32))
        ls = LnScratch(nc, ctx, pfx + "ln")

        bU = Buf()
        bX, bxn, bwsg, bparams = bU, Buf(), bU, Buf()
        bwf = [Buf(), Buf()]
        babsm, bmaxrow, bsg = Buf(), Buf(), Buf()
        bstage = [Buf(), Buf()]
        bresd = [Buf(), Buf()]
        bpsout = [Buf(), Buf()]
        bpstr, bpsb1 = Buf(), Buf()

        load_stream(P, nc, X, bX, x_in)
        for nm, dst in (("gamma", gam), ("beta", bet), ("lnw", lw), ("lnb", lb)):
            src = {"gamma": gamma, "beta": beta, "lnw": lnw, "lnb": lnb}[nm]
            P.op("sp", lambda s, dst=dst, src=src: s.dma_start(out=dst[:], in_=src),
                 outs=[bparams], dma=True)

        emit_ln(P, cm, X, bX, TPC, xn, bxn, ls)
        emit_ln(P, cm, xn, bxn, TPC, xn, bxn, ls,
                gamma_sb=lw, beta_sb=lb, bparams=bparams)

        hidx = 0
        for kc in range(NDC):
            for hf in range(2):
                f = hidx % 2
                P.op("sp", lambda s, kc=kc, hf=hf, f=f: s.dma_start(
                    out=wf[f][:],
                    in_=wT[kc * 128:(kc + 1) * 128, hf * 1024:(hf + 1) * 1024]),
                    outs=[bwf[f]], dma=True)
                P.op("act", lambda s, kc=kc, hf=hf, f=f: s.sign(
                    out=wsg[:, kc * D + hf * 1024: kc * D + (hf + 1) * 1024],
                    in_=wf[f][:]), ins=[bwf[f]], outs=[bwsg])
                col = absm[:, 0:1] if hidx == 0 else absm[:, 1:2]
                P.op("dve", lambda v, f=f, col=col: v.reduce_max(
                    col, wf[f][:], axis=AX.X, apply_absolute_value=True),
                    ins=[bwf[f]], outs=[babsm])
                if hidx > 0:
                    P.op("dve", lambda v: v.tensor_max(
                        absm[:, 0:1], absm[:, 0:1], absm[:, 1:2]),
                        ins=[babsm], outs=[babsm])
                hidx += 1

        P.op("pe", lambda pe: pe.transpose(pstr[:], absm[:, 0:1], cm.ident_f[:]),
             ins=[babsm], outs=[bpstr], waits=cm.ready)
        P.op("dve", lambda v: v.reduce_max(maxrow[:], pstr[:], axis=AX.X),
             ins=[bpstr], outs=[bmaxrow])
        P.op("pe", lambda pe: pe.matmul(
            psb1[:], cm.ones_row_f[:], maxrow[:], start=True, stop=True),
            ins=[bmaxrow], outs=[bpsb1])
        P.op("dve", lambda v: v.tensor_scalar(
            out=sg[:], in0=gam[:], scalar1=psb1[:], scalar2=None, op0=ALU.mult),
            ins=[bpsb1, bparams], outs=[bsg])

        si = 0
        for oc in range(NDC):
            for tcc in range(4):
                pb, bpb = psout[si % 2], bpsout[si % 2]
                stg, bstg = stage[si % 2], bstage[si % 2]
                for kc in range(NDC):
                    P.op("pe", lambda pe, oc=oc, tcc=tcc, kc=kc, pb=pb: pe.matmul(
                        pb[:], wsg[:, kc * D + oc * 128: kc * D + oc * 128 + 128],
                        xn[:, kc * TPC + tcc * 512: kc * TPC + tcc * 512 + 512],
                        start=(kc == 0), stop=(kc == NDC - 1)),
                        ins=[bwsg, bxn], outs=[bpb])
                P.op("dve", lambda v, pb=pb, stg=stg, oc=oc: v.tensor_scalar(
                    out=stg[:], in0=pb[:], scalar1=sg[:, oc:oc + 1],
                    scalar2=bet[:, oc:oc + 1], op0=ALU.mult, op1=ALU.add),
                    ins=[bpb, bsg, bparams], outs=[bstg])
                rb, brb = resd[si % 2], bresd[si % 2]
                P.op("sp", lambda s, rb=rb, oc=oc, tcc=tcc: s.dma_start(
                    out=rb[:],
                    in_=x_in[oc * 128:(oc + 1) * 128, tcc * 512:(tcc + 1) * 512]),
                    outs=[brb], dma=True)
                P.op("dve", lambda v, stg=stg, rb=rb: v.tensor_add(
                    stg[:], stg[:], rb[:]), ins=[brb], outs=[bstg])
                P.op("sp", lambda s, stg=stg, oc=oc, tcc=tcc: s.dma_start(
                    out=x_out[oc * 128:(oc + 1) * 128,
                              tcc * 512:(tcc + 1) * 512],
                    in_=stg[:]), ins=[bstg], dma=True)
                si += 1

        P.raw_op("sp", lambda s: s.nop(), waits=[P.tok("sp")])


def vec_pd(v):
    return np.ascontiguousarray(v.reshape(NDC, 128).T)


def shard_x(x):
    out = []
    for c in range(NCORES):
        b, r = c // 4, c % 4
        xc = np.concatenate([x[b, r * BLK:(r + 1) * BLK],
                             x[b, (7 - r) * BLK:(8 - r) * BLK]], axis=0)
        out.append(np.ascontiguousarray(xc.T))
    return out


def unshard_x(parts):
    x = np.empty((2, 8192, D), dtype=parts[0].dtype)
    for c in range(NCORES):
        b, r = c // 4, c % 4
        xc = parts[c].T
        x[b, r * BLK:(r + 1) * BLK] = xc[0:BLK]
        x[b, (7 - r) * BLK:(8 - r) * BLK] = xc[BLK:]
    return x


MOE_INPUTS = {"rwT": [D, 16], "rbias": [128, 16], "Wu": [16, D, 1024],
              "Wg": [16, D, 1024], "Wd": [16, 1024, D], "Su": [D, D],
              "Sg": [D, D], "Sd": [D, D]}


def emit_moe(P, nc, cm, A, pfx="mo"):
    x_in, x_out = A["x_in"], A["x_out"]
    rwT, rbias = A["rwT"], A["rbias"]
    Wu, Wg, Wd = A["Wu"], A["Wg"], A["Wd"]
    Su, Sg, Sd = A["Su"], A["Sg"], A["Sd"]
    with ExitStack() as ctx:
        e = ctx.enter_context
        U = e(nc.sbuf_tensor(pfx + "U", [128, NDC * TPC], BF16))
        X = U
        xln = e(nc.sbuf_tensor(pfx + "xln", [128, NDC * TPC], BF16))
        rw = e(nc.sbuf_tensor(pfx + "rw", [128, NDC * 16], BF16))
        rb = e(nc.sbuf_tensor(pfx + "rb", [128, 16], F32))
        iot16 = e(nc.sbuf_tensor(pfx + "iot16", [128, 16], F32))
        sel16 = e(nc.sbuf_tensor(pfx + "sel16", [16, 16 * 128], F32))
        lg = e(nc.sbuf_tensor(pfx + "lg", [128, 16], F32))
        zz = e(nc.sbuf_tensor(pfx + "zz", [128, 16], F32))
        eq = e(nc.sbuf_tensor(pfx + "eq", [128, 16], F32))
        tmp16 = e(nc.sbuf_tensor(pfx + "tmp16", [128, 16], F32))
        oh1 = e(nc.sbuf_tensor(pfx + "oh1", [128, 16], F32))
        oh2 = e(nc.sbuf_tensor(pfx + "oh2", [128, 16], F32))
        comb_tm = e(nc.sbuf_tensor(pfx + "comb_tm", [128, 16], F32))
        mcol = [e(nc.sbuf_tensor(pfx + f"mcol{i}", [128, 1], F32)) for i in range(6)]
        comb_fm = e(nc.sbuf_tensor(pfx + "comb_fm", [16, TPC], F32))
        comb_bc = e(nc.sbuf_tensor(pfx + "comb_bc", [128, TPC], BF16))
        wu_sb = [e(nc.sbuf_tensor(pfx + f"wu{i}", [128, NDC * 128], BF16)) for i in range(2)]
        wg_sb = [e(nc.sbuf_tensor(pfx + f"wg{i}", [128, NDC * 128], BF16)) for i in range(2)]
        hsc = e(nc.sbuf_tensor(pfx + "hsc", [128, 512], BF16))
        stage = [e(nc.sbuf_tensor(pfx + f"stg{i}", [128, TPC], F32)) for i in range(2)]
        ls = LnScratch(nc, ctx, pfx + "ln")
        ps_hu = e(nc.psum_tensor(pfx + "ps_hu", [128, 512], F32))
        ps_hg = e(nc.psum_tensor(pfx + "ps_hg", [128, 512], F32))
        ps_dn = [e(nc.psum_tensor(pfx + f"ps_dn{i}", [128, 512], F32)) for i in range(2)]

        bU, bxln, brw, brb = Buf(), Buf(), Buf(), Buf()
        biot, bsel = Buf(), Buf()
        blg, bzz, beq, btmp, boh1, boh2, bcomb_tm = (
            Buf(), Buf(), Buf(), Buf(), Buf(), Buf(), Buf())
        bm = [Buf() for _ in range(6)]
        bcomb_fm, bcomb_bc = Buf(), Buf()
        bwu = [Buf(), Buf()]
        bwg = [Buf(), Buf()]
        bhsc = Buf()
        bstage = [Buf(), Buf()]
        bps_hu, bps_hg = Buf(), Buf()
        bps_dn = [Buf(), Buf()]
        bpsbc = ls.b["psbc"]
        bh = [Buf() for _ in range(8)]
        bwd = [Buf() for _ in range(8)]
        bxout = [Buf() for _ in range(NDC)]

        load_stream(P, nc, X, bU, x_in)
        P.op("pool", lambda g: g.dma_start(
            out=rw[:], in_=rwT.rearrange("(a p) e -> p a e", p=128)),
            outs=[brw], dma=True)
        P.op("sp", lambda s: s.dma_start(out=rb[:], in_=rbias),
             outs=[brb], dma=True)
        P.op("pool", lambda g: g.iota(
            iot16[:], pattern=[[1, 16]], base=0, channel_multiplier=0,
            allow_small_or_imprecise_dtypes=True), outs=[biot])
        P.op("pool", lambda g: g.iota(
            sel16[:], pattern=[[1, 16], [0, 128]], base=0,
            channel_multiplier=-1, allow_small_or_imprecise_dtypes=True),
            outs=[bsel])
        P.op("dve", lambda v: v.tensor_scalar(
            out=sel16[:], in0=sel16[:], scalar1=0.0, scalar2=None,
            op0=ALU.is_equal), ins=[bsel], outs=[bsel])
        tinit = P.op("sp", lambda s: s.dma_start(out=x_out, in_=x_in), dma=True)
        for rbuf in bxout:
            rbuf.w = tinit

        emit_ln(P, cm, X, bU, TPC, xln, bxln, ls)
        for rbuf in bh + bwd:
            rbuf.r = [P.tok("pe"), P.tok("dve"), P.tok("act")]

        for tt in range(NDC):
            ps_r = ls.psbc[:, 0:16]
            def rmm(pe, tt=tt):
                last = None
                for dc in range(NDC):
                    last = pe.matmul(
                        ps_r, xln[:, dc * TPC + tt * 128: dc * TPC + tt * 128 + 128],
                        rw[:, dc * 16:(dc + 1) * 16],
                        start=(dc == 0), stop=(dc == NDC - 1))
                return last
            P.op("pe", rmm, ins=[bxln, brw], outs=[bpsbc], waits=cm.ready)
            P.op("dve", lambda v: v.tensor_copy(lg[:], ps_r), ins=[bpsbc], outs=[blg])
            P.op("dve", lambda v: v.tensor_add(zz[:], lg[:], rb[:]),
                 ins=[blg, brb], outs=[bzz])
            P.op("dve", lambda v: v.reduce_max(mcol[0][:], zz[:], axis=AX.X),
                 ins=[bzz], outs=[bm[0]])
            P.op("dve", lambda v: v.tensor_scalar(
                out=eq[:], in0=zz[:], scalar1=mcol[0][:], scalar2=None,
                op0=ALU.is_equal), ins=[bzz, bm[0]], outs=[beq])
            P.op("dve", lambda v: v.tensor_scalar(
                out=tmp16[:], in0=iot16[:], scalar1=1000.0, scalar2=None,
                op0=ALU.add), ins=[biot], outs=[btmp])
            P.op("dve", lambda v: v.scalar_tensor_tensor(
                out=tmp16[:], in0=eq[:], scalar=-1000.0, in1=tmp16[:],
                op0=ALU.mult, op1=ALU.add), ins=[beq], outs=[btmp])
            P.op("dve", lambda v: v.tensor_reduce(
                mcol[1][:], tmp16[:], axis=AX.X, op=ALU.min),
                ins=[btmp], outs=[bm[1]])
            P.op("dve", lambda v: v.tensor_scalar(
                out=oh1[:], in0=iot16[:], scalar1=mcol[1][:], scalar2=None,
                op0=ALU.is_equal), ins=[biot, bm[1]], outs=[boh1])
            P.op("dve", lambda v: v.tensor_mul(tmp16[:], lg[:], oh1[:]),
                 ins=[blg, boh1], outs=[btmp])
            P.op("dve", lambda v: v.reduce_sum(mcol[2][:], tmp16[:], axis=AX.X),
                 ins=[btmp], outs=[bm[2]])
            P.op("dve", lambda v: v.scalar_tensor_tensor(
                out=zz[:], in0=oh1[:], scalar=-30000.0, in1=zz[:],
                op0=ALU.mult, op1=ALU.add), ins=[boh1], outs=[bzz])
            P.op("dve", lambda v: v.reduce_max(mcol[0][:], zz[:], axis=AX.X),
                 ins=[bzz], outs=[bm[0]])
            P.op("dve", lambda v: v.tensor_scalar(
                out=eq[:], in0=zz[:], scalar1=mcol[0][:], scalar2=None,
                op0=ALU.is_equal), ins=[bzz, bm[0]], outs=[beq])
            P.op("dve", lambda v: v.tensor_scalar(
                out=tmp16[:], in0=iot16[:], scalar1=1000.0, scalar2=None,
                op0=ALU.add), ins=[biot], outs=[btmp])
            P.op("dve", lambda v: v.scalar_tensor_tensor(
                out=tmp16[:], in0=eq[:], scalar=-1000.0, in1=tmp16[:],
                op0=ALU.mult, op1=ALU.add), ins=[beq], outs=[btmp])
            P.op("dve", lambda v: v.tensor_reduce(
                mcol[1][:], tmp16[:], axis=AX.X, op=ALU.min),
                ins=[btmp], outs=[bm[1]])
            P.op("dve", lambda v: v.tensor_scalar(
                out=oh2[:], in0=iot16[:], scalar1=mcol[1][:], scalar2=None,
                op0=ALU.is_equal), ins=[biot, bm[1]], outs=[boh2])
            P.op("dve", lambda v: v.tensor_mul(tmp16[:], lg[:], oh2[:]),
                 ins=[blg, boh2], outs=[btmp])
            P.op("dve", lambda v: v.reduce_sum(mcol[3][:], tmp16[:], axis=AX.X),
                 ins=[btmp], outs=[bm[3]])
            P.op("dve", lambda v: v.tensor_max(mcol[4][:], mcol[2][:], mcol[3][:]),
                 ins=[bm[2], bm[3]], outs=[bm[4]])
            P.op("dve", lambda v: v.tensor_scalar(
                out=mcol[4][:], in0=mcol[4][:], scalar1=-1.0, scalar2=None,
                op0=ALU.mult), ins=[bm[4]], outs=[bm[4]])
            P.op("act", lambda s: s.activation(
                out=mcol[2][:], in_=mcol[2][:], func=AF.Exp, bias=mcol[4][:]),
                ins=[bm[2], bm[4]], outs=[bm[2]])
            P.op("act", lambda s: s.activation(
                out=mcol[3][:], in_=mcol[3][:], func=AF.Exp, bias=mcol[4][:]),
                ins=[bm[3], bm[4]], outs=[bm[3]])
            P.op("dve", lambda v: v.tensor_add(mcol[5][:], mcol[2][:], mcol[3][:]),
                 ins=[bm[2], bm[3]], outs=[bm[5]])
            P.op("dve", lambda v: v.reciprocal(mcol[5][:], mcol[5][:]),
                 ins=[bm[5]], outs=[bm[5]])
            P.op("dve", lambda v: v.tensor_mul(mcol[2][:], mcol[2][:], mcol[5][:]),
                 ins=[bm[2], bm[5]], outs=[bm[2]])
            P.op("dve", lambda v: v.tensor_mul(mcol[3][:], mcol[3][:], mcol[5][:]),
                 ins=[bm[3], bm[5]], outs=[bm[3]])
            P.op("dve", lambda v: v.tensor_scalar(
                out=comb_tm[:], in0=oh1[:], scalar1=mcol[2][:], scalar2=None,
                op0=ALU.mult), ins=[boh1, bm[2]], outs=[bcomb_tm])
            P.op("dve", lambda v: v.tensor_scalar(
                out=tmp16[:], in0=oh2[:], scalar1=mcol[3][:], scalar2=None,
                op0=ALU.mult), ins=[boh2, bm[3]], outs=[btmp])
            P.op("dve", lambda v: v.tensor_add(comb_tm[:], comb_tm[:], tmp16[:]),
                 ins=[btmp], outs=[bcomb_tm])
            P.op("pe", lambda pe: pe.transpose(
                ls.psbc[0:16, 0:128], comb_tm[:], cm.ident_f[:]),
                ins=[bcomb_tm], outs=[bpsbc], waits=cm.ready)
            P.op("dve", lambda v, tt=tt: v.tensor_copy(
                comb_fm[:, tt * 128:(tt + 1) * 128], ls.psbc[0:16, 0:128]),
                ins=[bpsbc], outs=[bcomb_fm])

        units = [(0, ee) for ee in range(16)] + [(1, 0), (1, 1)]
        for ui, (kind, idx) in enumerate(units):
            routed = (kind == 0)
            if routed:
                for tcc in range(4):
                    P.op("pe", lambda pe, idx=idx, tcc=tcc: pe.matmul(
                        ls.psbc[:], sel16[:, idx * 128:(idx + 1) * 128],
                        comb_fm[:, tcc * 512:(tcc + 1) * 512],
                        start=True, stop=True),
                        ins=[bsel, bcomb_fm], outs=[bpsbc])
                    P.op("dve", lambda v, tcc=tcc: v.tensor_copy(
                        comb_bc[:, tcc * 512:(tcc + 1) * 512], ls.psbc[:]),
                        ins=[bpsbc], outs=[bcomb_bc])
            for ec in range(8):
                f = ec % 2
                if routed:
                    usrc = Wu[idx, :, ec * 128:(ec + 1) * 128]
                    gsrc = Wg[idx, :, ec * 128:(ec + 1) * 128]
                    dsrc = Wd[idx, ec * 128:(ec + 1) * 128, :]
                else:
                    c0 = idx * 1024 + ec * 128
                    usrc = Su[:, c0:c0 + 128]
                    gsrc = Sg[:, c0:c0 + 128]
                    dsrc = Sd[c0:c0 + 128, :]
                P.op("pool", lambda g, usrc=usrc, f=f: g.dma_start(
                    out=wu_sb[f][:], in_=usrc.rearrange("(a p) c -> p a c", p=128)),
                    outs=[bwu[f]], dma=True)
                P.op("pool", lambda g, gsrc=gsrc, f=f: g.dma_start(
                    out=wg_sb[f][:], in_=gsrc.rearrange("(a p) c -> p a c", p=128)),
                    outs=[bwg[f]], dma=True)
                P.op("pool", lambda g, dsrc=dsrc, ec=ec: g.dma_start(
                    out=U[:, (8 + ec) * TPC:(9 + ec) * TPC], in_=dsrc),
                    outs=[bwd[ec]], dma=True)
                for tcc in range(4):
                    def umm(pe, f=f, tcc=tcc, w=wu_sb[f], ps=ps_hu):
                        last = None
                        for dc in range(NDC):
                            last = pe.matmul(
                                ps[:], w[:, dc * 128:(dc + 1) * 128],
                                xln[:, dc * TPC + tcc * 512: dc * TPC + tcc * 512 + 512],
                                start=(dc == 0), stop=(dc == NDC - 1))
                        return last
                    P.op("pe", umm, ins=[bwu[f], bxln], outs=[bps_hu])
                    def gmm(pe, f=f, tcc=tcc, w=wg_sb[f], ps=ps_hg):
                        last = None
                        for dc in range(NDC):
                            last = pe.matmul(
                                ps[:], w[:, dc * 128:(dc + 1) * 128],
                                xln[:, dc * TPC + tcc * 512: dc * TPC + tcc * 512 + 512],
                                start=(dc == 0), stop=(dc == NDC - 1))
                        return last
                    P.op("pe", gmm, ins=[bwg[f], bxln], outs=[bps_hg])
                    P.op("act", lambda s: s.activation(
                        out=hsc[:], in_=ps_hu[:], func=AF.Silu),
                        ins=[bps_hu], outs=[bhsc])
                    hdst = U[:, ec * TPC + tcc * 512: ec * TPC + tcc * 512 + 512]
                    if routed:
                        P.op("dve", lambda v, hdst=hdst: v.tensor_mul(
                            hdst, hsc[:], ps_hg[:]),
                            ins=[bhsc, bps_hg], outs=[bh[ec]])
                        P.op("dve", lambda v, hdst=hdst, tcc=tcc: v.tensor_mul(
                            hdst, hdst,
                            comb_bc[:, tcc * 512:(tcc + 1) * 512]),
                            ins=[bcomb_bc], outs=[bh[ec]])
                    else:
                        P.op("dve", lambda v, hdst=hdst: v.tensor_mul(
                            hdst, hsc[:], ps_hg[:]),
                            ins=[bhsc, bps_hg], outs=[bh[ec]])
            for dcout in range(NDC):
                sgb = stage[dcout % 2]
                bsgb = bstage[dcout % 2]
                for tcc in range(4):
                    pd = ps_dn[tcc % 2]
                    bpd = bps_dn[tcc % 2]
                    def dmm(pe, tcc=tcc, dcout=dcout, pd=pd):
                        last = None
                        for ec in range(8):
                            last = pe.matmul(
                                pd[:],
                                U[:, (8 + ec) * TPC + dcout * 128:
                                     (8 + ec) * TPC + dcout * 128 + 128],
                                U[:, ec * TPC + tcc * 512: ec * TPC + tcc * 512 + 512],
                                start=(ec == 0), stop=(ec == 7))
                        return last
                    P.op("pe", dmm, ins=bh + bwd, outs=[bpd])
                    P.op("dve", lambda v, sgb=sgb, tcc=tcc, pd=pd: v.tensor_copy(
                        sgb[:, tcc * 512:(tcc + 1) * 512], pd[:]),
                        ins=[bpd], outs=[bsgb])
                P.op("pool", lambda g, sgb=sgb, dcout=dcout: g.dma_start(
                    out=x_out[dcout * 128:(dcout + 1) * 128, :], in_=sgb[:],
                    accum_op=ALU.add), ins=[bsgb], outs=[bxout[dcout]], dma=True)

        P.raw_op("pool", lambda g: g.engine_nop(), waits=[P.tok("pool")])


MAMBA_INPUTS = {"WxT_dt": [D, 128], "WxT_bc": [D, 64], "WdtT": [128, D],
                "bdt": [128, NDC], "Apd": [128, NDC * 16], "Dpd": [128, NDC],
                "mask0": [128, 8], "mask1": [128, 8]}


def emit_mamba(P, nc, cm, A, pfx="mb"):
    x_in, x_out = A["x_in"], A["x_out"]
    WxT_dt, WxT_bc, WdtT = A["WxT_dt"], A["WxT_bc"], A["WdtT"]
    bdt, Apd, Dpd = A["bdt"], A["Apd"], A["Dpd"]
    mask0, mask1 = A["mask0"], A["mask1"]
    cc_in = nc.dram_tensor(pfx + "cc_in", [128, 1024], F32, kind="Internal")
    cc_out = nc.dram_tensor(pfx + "cc_out", [512, 1024], F32, kind="Internal")

    GL = 1024
    NG = 2

    with ExitStack() as ctx:
        e = ctx.enter_context
        U = e(nc.sbuf_tensor(pfx + "U", [128, NDC * TPC], BF16))
        X = U
        xln = e(nc.sbuf_tensor(pfx + "xln", [128, NDC * TPC], BF16))
        ov = e(nc.sbuf_tensor(pfx + "ov", [128, 3072], BF16))
        wxd = ov[:, 0:2048]
        wxbc = ov[:, 2048:3072]
        wdt = e(nc.sbuf_tensor(pfx + "wdt", [128, NDC * 128], BF16))
        bdtp = e(nc.sbuf_tensor(pfx + "bdtp", [128, NDC], F32))
        dpd_sb = e(nc.sbuf_tensor(pfx + "dpd_sb", [128, NDC], F32))
        aneg = e(nc.sbuf_tensor(pfx + "aneg", [128, NDC * 16], F32))
        selb = e(nc.sbuf_tensor(pfx + "selb", [64, 16 * 128], BF16))
        selc = e(nc.sbuf_tensor(pfx + "selc", [64, 16 * 128], BF16))
        bbc = U[:, 0:16 * GL]
        cbc = U[:, 16 * GL:32 * GL]
        bc4 = e(nc.sbuf_tensor(pfx + "bc4", [64, NG * GL], BF16))
        dtraw = e(nc.sbuf_tensor(pfx + "dtraw", [128, GL], BF16))
        zt = e(nc.sbuf_tensor(pfx + "zt", [128, GL], F32))
        dt_sb = e(nc.sbuf_tensor(pfx + "dt_sb", [128, GL], BF16))
        S_sb = zt
        dtx = e(nc.sbuf_tensor(pfx + "dtx", [128, GL], BF16))
        da = [e(nc.sbuf_tensor(pfx + f"da{i}", [128, GL], BF16)) for i in range(2)]
        dbx = [e(nc.sbuf_tensor(pfx + f"dbx{i}", [128, GL], BF16)) for i in range(2)]
        h_sb = [e(nc.sbuf_tensor(pfx + f"h_sb{i}", [128, GL], BF16)) for i in range(2)]
        _tmp0 = e(nc.sbuf_tensor(pfx + "tmp0", [128, GL], BF16))
        tmp = [_tmp0, _tmp0]
        y_acc = e(nc.sbuf_tensor(pfx + "y_acc", [128, GL], BF16))
        stageA = [e(nc.sbuf_tensor(pfx + "stA0", [128, GL], F32))] * 2
        t3 = stageA[0]
        hp = stageA[0][:, 0:512]
        S_tot = e(nc.sbuf_tensor(pfx + "S_tot", [128, NG * NDC], F32))
        Hc = e(nc.sbuf_tensor(pfx + "Hc", [128, 256], F32))
        summH = e(nc.sbuf_tensor(pfx + "summH", [128, 512], F32))
        summP = e(nc.sbuf_tensor(pfx + "summP", [128, 512], F32))
        h_run = Hc
        hi_all = e(nc.sbuf_tensor(pfx + "hi_all", [128, 8 * 256], F32))
        hi_t = e(nc.sbuf_tensor(pfx + "hi", [128, 256], F32))
        hi = [hi_t, hi_t]
        msk = [e(nc.sbuf_tensor(pfx + f"msk{i}", [128, 8], F32)) for i in range(2)]
        ls = LnScratch(nc, ctx, pfx + "ln")
        ps_a = [e(nc.psum_tensor(pfx + f"ps_a{i}", [128, 512], F32)) for i in range(2)]
        ps_b = [e(nc.psum_tensor(pfx + f"ps_b{i}", [128, 512], F32)) for i in range(2)]

        bU, bxln = Buf(), Buf()
        bpar = Buf()
        bsel = Buf()
        bbc4, bdtraw4 = Buf(), Buf()
        bbbc, bcbc = Buf(), Buf()
        bzt, bdt_sb, bdtx, byacc = (Buf() for _ in range(4))
        bS = bzt
        bda = [Buf(), Buf()]
        bdbx = [Buf(), Buf()]
        bh = [Buf(), Buf()]
        _btmp0 = Buf()
        btmp = [_btmp0, _btmp0]
        _bstA0 = Buf()
        bstA = [_bstA0, _bstA0]
        bt3 = _bstA0
        bhp = _bstA0
        bStot, bHc, bsumm, bhiall = (Buf() for _ in range(4))
        bhrun = bHc
        _bhi = Buf()
        bhi = [_bhi, _bhi]
        bmsk = [Buf(), Buf()]
        bps_a = [Buf(), Buf()]
        bps_b = [Buf(), Buf()]
        bxout = [Buf() for _ in range(NG * 16)]

        load_stream(P, nc, X, bU, x_in)
        P.op("pool", lambda g: g.dma_start(
            out=wxd[:], in_=WxT_dt.rearrange("(a p) c -> p a c", p=128)),
            outs=[bpar], dma=True)
        P.op("pool", lambda g: g.dma_start(
            out=wxbc[:], in_=WxT_bc.rearrange("(a p) c -> p a c", p=128)),
            outs=[bpar], dma=True)
        P.op("pool", lambda g: g.dma_start(
            out=wdt[:], in_=WdtT), outs=[bpar], dma=True)
        for nm, dst in (("bdt", bdtp), ("Apd", aneg), ("Dpd", dpd_sb),
                        ("mask0", msk[0]), ("mask1", msk[1])):
            srcs = {"bdt": bdt, "Apd": Apd, "Dpd": Dpd,
                    "mask0": mask0, "mask1": mask1}
            P.op("sp", lambda s, dst=dst, src=srcs[nm]: s.dma_start(
                out=dst[:], in_=src), outs=[bpar], dma=True)
        P.op("act", lambda s: s.activation(out=aneg[:], in_=aneg[:], func=AF.Exp),
             ins=[bpar], outs=[bpar])
        P.op("dve", lambda v: v.tensor_scalar(
            out=aneg[:], in0=aneg[:], scalar1=-1.0, scalar2=None, op0=ALU.mult),
            ins=[bpar], outs=[bpar])
        P.op("pool", lambda g: g.iota(
            selb[:], pattern=[[1, 16], [0, 128]], base=0, channel_multiplier=-1,
            allow_small_or_imprecise_dtypes=True), outs=[bsel])
        P.op("dve", lambda v: v.tensor_scalar(
            out=selb[:], in0=selb[:], scalar1=0.0, scalar2=None, op0=ALU.is_equal),
            ins=[bsel], outs=[bsel])
        P.op("pool", lambda g: g.iota(
            selc[:], pattern=[[1, 16], [0, 128]], base=32, channel_multiplier=-1,
            allow_small_or_imprecise_dtypes=True), outs=[bsel])
        P.op("dve", lambda v: v.tensor_scalar(
            out=selc[:], in0=selc[:], scalar1=0.0, scalar2=None, op0=ALU.is_equal),
            ins=[bsel], outs=[bsel])
        P.op("dve", lambda v: v.memset(Hc[:], 0.0), outs=[bHc])
        tinit = P.op("sp", lambda s: s.dma_start(out=x_out, in_=x_in), dma=True)
        for rb in bxout:
            rb.w = tinit

        emit_ln(P, cm, X, bU, TPC, xln, bxln, ls)
        for rb in (bbbc, bcbc):
            rb.r = [P.tok("pe"), P.tok("dve"), P.tok("act")]

        for g in range(NG):
            for hf in range(2):
                t0 = g * GL + hf * 512
                pb, bpb = ps_b[hf], bps_b[hf]
                def mm_bc(pe, t0=t0, pb=pb):
                    last = None
                    for dc in range(NDC):
                        last = pe.matmul(
                            pb[0:64, :], wxbc[:, dc * 64:(dc + 1) * 64],
                            xln[:, dc * TPC + t0: dc * TPC + t0 + 512],
                            start=(dc == 0), stop=(dc == NDC - 1))
                    return last
                P.op("pe", mm_bc, ins=[bpar, bxln], outs=[bpb], waits=cm.ready)
                P.op("dve", lambda v, g=g, hf=hf, pb=pb: v.tensor_copy(
                    bc4[0:64, g * GL + hf * 512: g * GL + (hf + 1) * 512],
                    pb[0:64, :]), ins=[bpb], outs=[bbc4])

        def xz_dtraw(g):
            for hf in range(2):
                t0 = g * GL + hf * 512
                pa, bpa = ps_a[hf], bps_a[hf]
                def mm_dt(pe, t0=t0, pa=pa):
                    last = None
                    for dc in range(NDC):
                        last = pe.matmul(
                            pa[:], wxd[:, dc * 128:(dc + 1) * 128],
                            xln[:, dc * TPC + t0: dc * TPC + t0 + 512],
                            start=(dc == 0), stop=(dc == NDC - 1))
                    return last
                P.op("pe", mm_dt, ins=[bpar, bxln], outs=[bpa], waits=cm.ready)
                P.op("dve", lambda v, hf=hf, pa=pa: v.tensor_copy(
                    dtraw[:, hf * 512:(hf + 1) * 512], pa[:]),
                    ins=[bpa], outs=[bdtraw4])

        def group_pre(g, dc, save_S=False, want_dtx=False):
            t0 = g * GL
            for hf in range(2):
                pa, bpa = ps_a[hf], bps_a[hf]
                P.op("pe", lambda pe, dc=dc, hf=hf, pa=pa: pe.matmul(
                    pa[:], wdt[:, dc * 128:(dc + 1) * 128],
                    dtraw[:, hf * 512:(hf + 1) * 512],
                    start=True, stop=True),
                    ins=[bpar, bdtraw4], outs=[bpa])
                P.op("dve", lambda v, dc=dc, hf=hf, pa=pa: v.tensor_scalar(
                    out=zt[:, hf * 512:(hf + 1) * 512], in0=pa[:],
                    scalar1=bdtp[:, dc:dc + 1], scalar2=None, op0=ALU.add),
                    ins=[bpa, bpar], outs=[bzt])
            P.op("act", lambda s: s.activation(out=t3[:], in_=zt[:], func=AF.Abs),
                 ins=[bzt], outs=[bt3])
            P.op("act", lambda s: s.activation(
                out=t3[:], in_=t3[:], func=AF.Exp, scale=-1.0),
                ins=[bt3], outs=[bt3])
            P.op("act", lambda s: s.activation(
                out=t3[:], in_=t3[:], func=AF.Ln, bias=1.0),
                ins=[bt3], outs=[bt3])
            P.op("dve", lambda v: v.tensor_scalar(
                out=zt[:], in0=zt[:], scalar1=0.0, scalar2=None, op0=ALU.max),
                ins=[bzt], outs=[bzt])
            P.op("dve", lambda v: v.tensor_add(dt_sb[:], zt[:], t3[:]),
                 ins=[bzt, bt3], outs=[bdt_sb])
            P.op("dve", lambda v: v.tensor_tensor_scan(
                out=S_sb[:], data0=dt_sb[:], data1=dt_sb[:], initial=0.0,
                op0=ALU.add, op1=ALU.bypass),
                ins=[bdt_sb], outs=[bS])
            if save_S:
                P.op("dve", lambda v, g=g, dc=dc: v.tensor_copy(
                    S_tot[:, g * 16 + dc: g * 16 + dc + 1], S_sb[:, GL - 1:GL]),
                    ins=[bS], outs=[bStot])
            if want_dtx:
                P.op("dve", lambda v, dc=dc, t0=t0: v.tensor_mul(
                    dtx[:], dt_sb[:],
                    xln[:, dc * TPC + t0: dc * TPC + t0 + GL]),
                    ins=[bdt_sb, bxln], outs=[bdtx])

        def bcast_bc(g, want_b=True):
            for n in range(16):
                for hf in range(2):
                    pa, bpa = ps_a[hf], bps_a[hf]
                    pb, bpb = ps_b[hf], bps_b[hf]
                    if want_b:
                        P.op("pe", lambda pe, n=n, g=g, hf=hf, pa=pa: pe.matmul(
                            pa[:], selb[:, n * 128:(n + 1) * 128],
                            bc4[0:64, g * GL + hf * 512: g * GL + (hf + 1) * 512],
                            start=True, stop=True),
                            ins=[bsel, bbc4], outs=[bpa])
                        P.op("dve", lambda v, n=n, hf=hf, pa=pa: v.tensor_copy(
                            bbc[:, n * GL + hf * 512: n * GL + (hf + 1) * 512],
                            pa[:]), ins=[bpa], outs=[bbbc])
                    P.op("pe", lambda pe, n=n, g=g, hf=hf, pb=pb: pe.matmul(
                        pb[:], selc[:, n * 128:(n + 1) * 128],
                        bc4[0:64, g * GL + hf * 512: g * GL + (hf + 1) * 512],
                        start=True, stop=True),
                        ins=[bsel, bbc4], outs=[bpb])
                    P.op("dve", lambda v, n=n, hf=hf, pb=pb: v.tensor_copy(
                        cbc[:, n * GL + hf * 512: n * GL + (hf + 1) * 512],
                        pb[:]), ins=[bpb], outs=[bcbc])

        for g in range(NG):
            sl = g
            t0 = g * GL
            xz_dtraw(g)
            bcast_bc(g, want_b=True)
            for dc in range(NDC):
                group_pre(g, dc, save_S=True, want_dtx=True)
                P.op("dve", lambda v: v.memset(y_acc[:], 0.0), outs=[byacc])
                for n in range(16):
                    f = n % 2
                    acol = aneg[:, dc * 16 + n: dc * 16 + n + 1]
                    P.op("act", lambda s, acol=acol, f=f: s.activation(
                        out=da[f][:], in_=dt_sb[:], func=AF.Exp, scale=acol),
                        ins=[bdt_sb, bpar], outs=[bda[f]])
                    P.op("pool", lambda g2, n=n, f=f: g2.tensor_mul(
                        dbx[f][:], dtx[:], bbc[:, n * GL:(n + 1) * GL]),
                        ins=[bdtx, bbbc], outs=[bdbx[f]])
                    P.op("dve", lambda v, f=f: v.tensor_tensor_scan(
                        out=h_sb[f][:], data0=da[f][:], data1=dbx[f][:],
                        initial=0.0, op0=ALU.mult, op1=ALU.add),
                        ins=[bda[f], bdbx[f]], outs=[bh[f]])
                    hcol = Hc[:, dc * 16 + n: dc * 16 + n + 1]
                    P.op("dve", lambda v, hcol=hcol, f=f: v.tensor_copy(
                        hcol, h_sb[f][:, GL - 1:GL]), ins=[bh[f]], outs=[bHc])
                    P.op("pool", lambda g2, n=n, f=f: g2.tensor_mul(
                        tmp[f][:], h_sb[f][:], cbc[:, n * GL:(n + 1) * GL]),
                        ins=[bh[f], bcbc], outs=[btmp[f]])
                    P.op("dve", lambda v, f=f: v.tensor_add(
                        y_acc[:], y_acc[:], tmp[f][:]),
                        ins=[btmp[f]], outs=[byacc])
                st, bst = stageA[dc % 2], bstA[dc % 2]
                P.op("dve", lambda v, st=st, dc=dc, t0=t0: v.scalar_tensor_tensor(
                    out=st[:], in0=xln[:, dc * TPC + t0: dc * TPC + t0 + GL],
                    scalar=dpd_sb[:, dc:dc + 1], in1=y_acc[:],
                    op0=ALU.mult, op1=ALU.add),
                    ins=[bxln, bpar, byacc], outs=[bst])
                P.op("pool", lambda g2, st=st, dc=dc, t0=t0: g2.dma_start(
                    out=x_out[dc * 128:(dc + 1) * 128, t0:t0 + GL], in_=st[:],
                    accum_op=ALU.add),
                    ins=[bst], outs=[bxout[g * 16 + dc]], dma=True)
                tot = S_tot[:, sl * 16 + dc: sl * 16 + dc + 1]
                P.op("dve", lambda v, dc=dc, tot=tot, sl=sl: v.tensor_scalar(
                    out=summP[:, sl * 256 + dc * 16: sl * 256 + (dc + 1) * 16],
                    in0=aneg[:, dc * 16:(dc + 1) * 16], scalar1=tot,
                    scalar2=None, op0=ALU.mult),
                    ins=[bpar, bStot], outs=[bsumm])
            P.op("dve", lambda v, sl=sl: v.tensor_copy(
                summH[:, sl * 256:(sl + 1) * 256], Hc[:]),
                ins=[bHc], outs=[bsumm])
        P.op("act", lambda s: s.activation(
            out=summP[:], in_=summP[:], func=AF.Exp),
            ins=[bsumm], outs=[bsumm])

        tst1 = P.op("pool", lambda g: g.dma_start(
            out=cc_in[:, 0:512], in_=summH[:]), ins=[bsumm], dma=True)
        tst2 = P.op("pool", lambda g: g.dma_start(
            out=cc_in[:, 512:1024], in_=summP[:]), ins=[bsumm], dma=True)
        tcc = P.op("pool", lambda g: g.collective_compute(
            "AllGather", ALU.bypass,
            replica_groups=[[0, 1, 2, 3], [4, 5, 6, 7]],
            ins=[cc_in.ap().opt()], outs=[cc_out.ap().opt()]),
            waits=[tst1, tst2], inc=1)
        P.op("dve", lambda v: v.memset(h_run[:], 0.0), outs=[bhrun])
        order = [(0, 0), (1, 0), (2, 0), (3, 0), (3, 1), (2, 1), (1, 1), (0, 1)]
        for m, (rr, ss) in enumerate(order):
            P.op("sp", lambda s, rr=rr, ss=ss: s.dma_start(
                out=hp[:, 0:256],
                in_=cc_out[rr * 128:(rr + 1) * 128, ss * 256:(ss + 1) * 256]),
                waits=[tcc], outs=[bhp], dma=True)
            P.op("sp", lambda s, rr=rr, ss=ss: s.dma_start(
                out=hp[:, 256:512],
                in_=cc_out[rr * 128:(rr + 1) * 128,
                           512 + ss * 256: 512 + (ss + 1) * 256]),
                waits=[tcc], outs=[bhp], dma=True)
            P.op("dve", lambda v, m=m: v.tensor_copy(
                hi_all[:, m * 256:(m + 1) * 256], h_run[:]),
                ins=[bhrun], outs=[bhiall])
            P.op("dve", lambda v: v.tensor_mul(
                h_run[:], h_run[:], hp[:, 256:512]),
                ins=[bhp], outs=[bhrun])
            P.op("dve", lambda v: v.tensor_add(
                h_run[:], h_run[:], hp[:, 0:256]),
                ins=[bhp], outs=[bhrun])
        for g in range(NG):
            sl = g
            t0 = g * GL
            P.op("dve", lambda v, sl=sl: v.memset(hi[sl][:], 0.0), outs=[bhi[sl]])
            for m in range(8):
                P.op("dve", lambda v, sl=sl, m=m: v.scalar_tensor_tensor(
                    out=hi[sl][:], in0=hi_all[:, m * 256:(m + 1) * 256],
                    scalar=msk[sl][:, m:m + 1], in1=hi[sl][:],
                    op0=ALU.mult, op1=ALU.add),
                    ins=[bhiall, bmsk[sl]], outs=[bhi[sl]])
            xz_dtraw(g)
            bcast_bc(g, want_b=False)
            for dc in range(NDC):
                group_pre(g, dc, save_S=False, want_dtx=False)
                P.op("dve", lambda v: v.memset(y_acc[:], 0.0), outs=[byacc])
                for n in range(16):
                    f = n % 2
                    acol = aneg[:, dc * 16 + n: dc * 16 + n + 1]
                    hcol = hi[sl][:, dc * 16 + n: dc * 16 + n + 1]
                    P.op("act", lambda s, acol=acol, f=f: s.activation(
                        out=da[f][:], in_=S_sb[:], func=AF.Exp, scale=acol),
                        ins=[bS, bpar], outs=[bda[f]])
                    P.op("pool", lambda g2, hcol=hcol, f=f: g2.tensor_scalar(
                        out=tmp[f][:], in0=da[f][:], scalar1=hcol, scalar2=None,
                        op0=ALU.mult), ins=[bda[f], bhi[sl]], outs=[btmp[f]])
                    P.op("dve", lambda v, n=n, f=f: v.tensor_mul(
                        tmp[f][:], tmp[f][:], cbc[:, n * GL:(n + 1) * GL]),
                        ins=[bcbc], outs=[btmp[f]])
                    P.op("dve", lambda v, f=f: v.tensor_add(
                        y_acc[:], y_acc[:], tmp[f][:]),
                        ins=[btmp[f]], outs=[byacc])
                st, bst = stageA[dc % 2], bstA[dc % 2]
                P.op("dve", lambda v, st=st: v.tensor_copy(st[:], y_acc[:]),
                     ins=[byacc], outs=[bst])
                P.op("pool", lambda g2, st=st, dc=dc, t0=t0: g2.dma_start(
                    out=x_out[dc * 128:(dc + 1) * 128, t0:t0 + GL], in_=st[:],
                    accum_op=ALU.add),
                    ins=[bst], outs=[bxout[g * 16 + dc]], dma=True)

        P.raw_op("pool", lambda g: g.engine_nop(), waits=[P.tok("pool")])
        P.raw_op("sp", lambda s: s.nop(), waits=[P.tok("sp")])

ATTN_LVL = 7
DBG_ATTN = False


ATTN_INPUTS = {"Wqkv": [D, 3 * D], "bq": [128, NDC], "bk": [128, NDC],
               "bv_row": [1, D], "Wout": [512, D], "bout": [128, NDC]}
ATTN_INPUTS_I32 = {"qidx": [128, 32], "kidx": [128, 32], "vidx": [128, 64]}


def emit_attn(P, nc, cm, A, pfx="at"):
    x_in, x_out = A["x_in"], A["x_out"]
    Wqkv, bq, bk, bv_row = A["Wqkv"], A["bq"], A["bk"], A["bv_row"]
    Wout, bout = A["Wout"], A["bout"]
    qidx, kidx, vidx = A["qidx"], A["kidx"], A["vidx"]
    ag_in = nc.dram_tensor(pfx + "ag_in", [3 * D, TPC], BF16, kind="Internal")
    ag_out = nc.dram_tensor(pfx + "ag_out", [4 * 3 * D, TPC], BF16, kind="Internal")
    rs_in = nc.dram_tensor(pfx + "rs_in", [4 * D, TPC], F32, kind="Internal")
    rs_out = nc.dram_tensor(pfx + "rs_out", [D, TPC], F32, kind="Internal")
    ag_view = ag_out.ap().rearrange("r (a c) -> (r a) c", c=128)

    ISQ = 0.08838834764831845
    with ExitStack() as ctx:
        e = ctx.enter_context
        U1 = e(nc.sbuf_tensor(pfx + "U1", [128, NDC * TPC], BF16))
        xln = U1
        U2 = e(nc.sbuf_tensor(pfx + "U2", [128, NDC * TPC], BF16))
        X = U2
        wq_s = [U2[:, 0:2048], U2[:, 2048:4096]]
        wv_s = U2[:, 4096:12288]
        bvbc = U2[:, 12288:14336]
        qkstg = [U2[:, 14336 + i * 512:14336 + (i + 1) * 512] for i in range(2)]
        V_sb = U2[:, 0:32768]
        wout_sb = U2[:, 0:8192]
        K_sb = e(nc.sbuf_tensor(pfx + "K_sb", [128, 8192], BF16))
        Q_sb = e(nc.sbuf_tensor(pfx + "Q_sb", [128, 1024], BF16))
        P_sb = [e(nc.sbuf_tensor(pfx + f"P_sb{i}", [128, 1024], BF16)) for i in range(2)]
        rl_row = e(nc.sbuf_tensor(pfx + "rl_row", [1, 1024], F32))
        rl_bc = e(nc.sbuf_tensor(pfx + "rl_bc", [128, 1024], BF16))
        m01 = e(nc.sbuf_tensor(pfx + "m01", [128, 2048], BF16))
        m01f = e(nc.sbuf_tensor(pfx + "m01f", [128, 2048], F32))
        idxs = e(nc.sbuf_tensor(pfx + "idxs", [128, 128], mybir.dt.int32))
        bqs = e(nc.sbuf_tensor(pfx + "bqs", [128, NDC], F32))
        bks = e(nc.sbuf_tensor(pfx + "bks", [128, NDC], F32))
        bvs = e(nc.sbuf_tensor(pfx + "bvs", [1, D], F32))
        bos = e(nc.sbuf_tensor(pfx + "bos", [128, NDC], F32))
        stg32 = [e(nc.sbuf_tensor(pfx + f"stg32_{i}", [128, 512], F32)) for i in range(2)]
        btile = e(nc.sbuf_tensor(pfx + "btile", [128, TPC], F32))
        ls = LnScratch(nc, ctx, pfx + "ln")
        ps_q = [e(nc.psum_tensor(pfx + f"ps_q{i}", [128, 512], F32)) for i in range(2)]
        ps_s = [e(nc.psum_tensor(pfx + f"ps_s{i}", [128, 512], F32)) for i in range(2)]
        ps_ctx = ps_q
        ps_l = [ls.psA, ls.psB]

        bU1, bU2 = Buf(), Buf()
        bpar, bidx = Buf(), Buf()
        bK, bQ = Buf(), Buf()
        bP = [Buf(), Buf()]
        brl_row, brl_bc, bm01 = Buf(), Buf(), Buf()
        bstg = [Buf(), Buf()]
        bps_q = [Buf(), Buf()]
        bps_s = [Buf(), Buf()]
        bps_ctx = bps_q
        bps_l = [ls.b["psA"], ls.b["psB"]]
        bbtile = Buf()
        bagin, bagout, brsin = Buf(), Buf(), Buf()
        bxout = Buf()

        load_stream(P, nc, X, bU2, x_in)
        for nm, dst in (("bq", bqs), ("bk", bks), ("bv", bvs), ("bo", bos),
                        ("ix", idxs)):
            src = {"bq": bq, "bk": bk, "bv": bv_row, "bo": bout, "ix": None}[nm]
            if nm == "ix":
                continue
            P.op("sp", lambda s, dst=dst, src=src: s.dma_start(out=dst[:], in_=src),
                 outs=[bpar], dma=True)
        P.op("sp", lambda s: s.dma_start(out=idxs[:, 0:32], in_=qidx),
             outs=[bidx], dma=True)
        P.op("sp", lambda s: s.dma_start(out=idxs[:, 32:64], in_=kidx),
             outs=[bidx], dma=True)
        P.op("sp", lambda s: s.dma_start(out=idxs[:, 64:128], in_=vidx),
             outs=[bidx], dma=True)
        P.op("pool", lambda g: g.iota(
            m01f[:], pattern=[[1, 2048]], base=-1024, channel_multiplier=-1,
            allow_small_or_imprecise_dtypes=True), outs=[bm01])
        P.op("dve", lambda v: v.tensor_scalar(
            out=m01[:], in0=m01f[:], scalar1=0.0, scalar2=None, op0=ALU.is_ge),
            ins=[bm01], outs=[bm01])

        emit_ln(P, cm, X, bU2, TPC, xln, bU1, ls)
        bU2.r.extend([P.tok("pe"), P.tok("dve"), P.tok("act")])

        for part, bias_sb, scale in ((0, bqs, ISQ), (1, bks, None)):
            for jt in range(NDC):
                f = jt % 2
                wsrc = Wqkv[:, part * D + jt * 128: part * D + (jt + 1) * 128]
                P.op("pool", lambda g, wsrc=wsrc, f=f: g.dma_start(
                    out=wq_s[f], in_=wsrc.rearrange("(a p) c -> p a c", p=128)),
                    outs=[bU2], dma=True)
                for tcc in range(4):
                    pq, bpq = ps_q[tcc % 2], bps_q[tcc % 2]
                    def qmm(pe, f=f, tcc=tcc, pq=pq):
                        last = None
                        for dc in range(NDC):
                            last = pe.matmul(
                                pq[:], wq_s[f][:, dc * 128:(dc + 1) * 128],
                                xln[:, dc * TPC + tcc * 512: dc * TPC + tcc * 512 + 512],
                                start=(dc == 0), stop=(dc == NDC - 1))
                        return last
                    P.op("pe", qmm, ins=[bU2, bU1], outs=[bpq], waits=cm.ready)
                    stg = qkstg[tcc % 2]
                    if scale is not None:
                        P.op("dve", lambda v, pq=pq, stg=stg, jt=jt, bias_sb=bias_sb:
                             v.tensor_scalar(
                                 out=stg, in0=pq[:], scalar1=bias_sb[:, jt:jt + 1],
                                 scalar2=ISQ, op0=ALU.add, op1=ALU.mult),
                             ins=[bpq, bpar], outs=[bU2])
                    else:
                        P.op("dve", lambda v, pq=pq, stg=stg, jt=jt, bias_sb=bias_sb:
                             v.tensor_scalar(
                                 out=stg, in0=pq[:], scalar1=bias_sb[:, jt:jt + 1],
                                 scalar2=None, op0=ALU.add),
                             ins=[bpq, bpar], outs=[bU2])
                    P.op("sp", lambda s, stg=stg, part=part, jt=jt, tcc=tcc:
                         s.dma_start(
                             out=ag_in[part * D + jt * 128:
                                       part * D + (jt + 1) * 128,
                                       tcc * 512:(tcc + 1) * 512], in_=stg),
                         ins=[bU2], outs=[bagin], dma=True)
        for vc in range(4):
            P.op("pe", lambda pe, vc=vc: pe.matmul(
                ls.psbc[:], cm.ones_row_f[:], bvs[0:1, vc * 512:(vc + 1) * 512],
                start=True, stop=True),
                ins=[bpar], outs=[ls.b["psbc"]], waits=cm.ready)
            P.op("dve", lambda v, vc=vc: v.tensor_copy(
                bvbc[:, vc * 512:(vc + 1) * 512]
                if False else U2[:, 12288 + vc * 512:12288 + (vc + 1) * 512],
                ls.psbc[:]), ins=[ls.b["psbc"]], outs=[bU2])
        for vc in range(4):
            wsrc = Wqkv[:, 2 * D + vc * 512: 2 * D + (vc + 1) * 512]
            P.op("pool", lambda g, wsrc=wsrc: g.dma_start(
                out=wv_s, in_=wsrc.rearrange("(a p) c -> p a c", p=128)),
                outs=[bU2], dma=True)
            for tt in range(NDC):
                pq, bpq = ps_q[tt % 2], bps_q[tt % 2]
                def vmm(pe, vc=vc, tt=tt, pq=pq):
                    last = None
                    for dc in range(NDC):
                        last = pe.matmul(
                            pq[:], xln[:, dc * TPC + tt * 128: dc * TPC + tt * 128 + 128],
                            wv_s[:, dc * 512:(dc + 1) * 512],
                            start=(dc == 0), stop=(dc == NDC - 1))
                    return last
                P.op("pe", vmm, ins=[bU2, bU1], outs=[bpq])
                stg = qkstg[tt % 2]
                P.op("dve", lambda v, pq=pq, stg=stg, vc=vc: v.tensor_add(
                    stg, pq[:], U2[:, 12288 + vc * 512:12288 + (vc + 1) * 512]),
                    ins=[bpq], outs=[bU2])
                P.op("sp", lambda s, stg=stg, vc=vc, tt=tt: s.dma_start(
                    out=ag_in[2 * D + tt * 128: 2 * D + (tt + 1) * 128,
                              vc * 512:(vc + 1) * 512], in_=stg),
                    ins=[bU2], outs=[bagin], dma=True)

        tag = P.op("pool", lambda g: g.collective_compute(
            "AllGather", ALU.bypass,
            replica_groups=[[0, 1, 2, 3], [4, 5, 6, 7]],
            ins=[ag_in.ap().opt()], outs=[ag_out.ap().opt()]),
            waits=[P.tok("sp")], ins=[bagin], inc=1)
        bagout.w = tag


        vload = [None] * 64
        for j2 in range(8):
            for aa in range(8):
                col = j2 * 8 + aa
                t = P.op("pool", lambda g, col=col: g.indirect_dma_start(
                    out=V_sb[:, col * 512:(col + 1) * 512], out_offset=None,
                    in_=ag_view,
                    in_offset=bass.IndirectOffsetOnAxis(idxs[:, 64 + col:65 + col], 0)),
                    ins=[bidx], waits=[tag], outs=[bU2], dma=True)
                vload[col] = t

        for hl in range(4):
            for qb in range(8):
                P.op("pool", lambda g, hl=hl, qb=qb: g.indirect_dma_start(
                    out=Q_sb[:], out_offset=None, in_=ag_view,
                    in_offset=bass.IndirectOffsetOnAxis(
                        idxs[:, hl * 8 + qb: hl * 8 + qb + 1], 0)),
                    ins=[bidx], waits=[tag], outs=[bQ], dma=True)
                if hl == 0 or True:
                    pass
                if qb == 0:
                    krange = [0]
                else:
                    krange = [qb]
                for j2 in krange:
                    P.op("pool", lambda g, hl=hl, j2=j2: g.indirect_dma_start(
                        out=K_sb[:, j2 * 1024:(j2 + 1) * 1024], out_offset=None,
                        in_=ag_view,
                        in_offset=bass.IndirectOffsetOnAxis(
                            idxs[:, 32 + hl * 8 + j2: 32 + hl * 8 + j2 + 1], 0)),
                        ins=[bidx], waits=[tag], outs=[bK], dma=True)
                nkb = (qb + 1) * 8

                def stage1(kb):
                    psb, bpsb = P_sb[kb % 2], bP[kb % 2]
                    for half in range(2):
                        pss, bpss = ps_s[half], bps_s[half]
                        P.op("pe", lambda pe, kb=kb, half=half, pss=pss: pe.matmul(
                            pss[:], K_sb[:, kb * 128:(kb + 1) * 128],
                            Q_sb[:, half * 512:(half + 1) * 512],
                            start=True, stop=True),
                            ins=[bK, bQ], outs=[bpss])
                        P.op("act", lambda s, pss=pss, psb=psb, half=half:
                             s.activation(
                                 out=psb[:, half * 512:(half + 1) * 512],
                                 in_=pss[:], func=AF.Exp),
                             ins=[bpss], outs=[bpsb])

                def stage2(kb):
                    psb, bpsb = P_sb[kb % 2], bP[kb % 2]
                    if kb >= qb * 8:
                        off = 1024 - 128 * (kb - qb * 8)
                        P.op("dve", lambda v, psb=psb, off=off: v.tensor_mul(
                            psb[:], psb[:], m01[:, off:off + 1024]),
                            ins=[bm01], outs=[bpsb])
                    for half in range(2):
                        P.op("pe", lambda pe, kb=kb, half=half, psb=psb: pe.matmul(
                            ps_l[half][:], cm.ones_col_bf[:],
                            psb[:, half * 512:(half + 1) * 512],
                            start=(kb == 0), stop=(kb == nkb - 1)),
                            ins=[bpsb], outs=[bps_l[half]])
                        P.op("pe", lambda pe, kb=kb, half=half, psb=psb, hl=hl:
                             pe.matmul(
                                 ps_ctx[half][:], V_sb[:, kb * 512 + hl * 128:
                                                       kb * 512 + hl * 128 + 128],
                                 psb[:, half * 512:(half + 1) * 512],
                                 start=(kb == 0), stop=(kb == nkb - 1)),
                             ins=[bpsb, bU2], outs=[bps_ctx[half]])

                stage1(0)
                for kb in range(1, nkb):
                    stage1(kb)
                    stage2(kb - 1)
                stage2(nkb - 1)
                for half in range(2):
                    P.op("dve", lambda v, half=half: v.tensor_copy(
                        rl_row[0:1, half * 512:(half + 1) * 512], ps_l[half][:]),
                        ins=[bps_l[half]], outs=[brl_row])
                P.op("dve", lambda v: v.reciprocal(rl_row[:], rl_row[:]),
                     ins=[brl_row], outs=[brl_row])
                for half in range(2):
                    P.op("pe", lambda pe, half=half: pe.matmul(
                        ls.psbc[:], cm.ones_row_f[:],
                        rl_row[0:1, half * 512:(half + 1) * 512],
                        start=True, stop=True),
                        ins=[brl_row], outs=[ls.b["psbc"]])
                    P.op("dve", lambda v, half=half: v.tensor_copy(
                        rl_bc[:, half * 512:(half + 1) * 512], ls.psbc[:]),
                        ins=[ls.b["psbc"]], outs=[brl_bc])
                for half in range(2):
                    dst = U1[:, hl * 8192 + qb * 1024 + half * 512:
                             hl * 8192 + qb * 1024 + (half + 1) * 512]
                    P.op("dve", lambda v, dst=dst, half=half: v.tensor_mul(
                        dst, ps_ctx[half][:],
                        rl_bc[:, half * 512:(half + 1) * 512]),
                        ins=[bps_ctx[half], brl_bc], outs=[bU1])

        if DBG_ATTN:
            y_ctx = nc.dram_tensor("y_ctx", [512, 8192], BF16,
                                   kind="ExternalOutput").ap()
            for hl in range(4):
                P.op("sp", lambda s, hl=hl: s.dma_start(
                    out=y_ctx[hl * 128:(hl + 1) * 128, :],
                    in_=U1[:, hl * 8192:(hl + 1) * 8192]),
                    ins=[bU1], dma=True)
        P.op("pool", lambda g: g.dma_start(
            out=wout_sb, in_=Wout.rearrange("(a p) c -> p a c", p=128)),
            outs=[bU2], dma=True)
        tinit = P.op("sp", lambda s: s.dma_start(out=x_out, in_=x_in), dma=True)
        bxout.w = tinit
        for dq in range(NDC):
            for tg in range(16):
                qb = tg // 2
                sect = owner(qb)
                lcol = slot(qb) * 1024 + (tg % 2) * 512
                pq, bpq = ps_q[tg % 2], bps_q[tg % 2]
                def omm(pe, dq=dq, tg=tg, pq=pq):
                    last = None
                    for hl in range(4):
                        last = pe.matmul(
                            pq[:],
                            U2[:, hl * 2048 + dq * 128:
                                  hl * 2048 + dq * 128 + 128],
                            U1[:, hl * 8192 + tg * 512: hl * 8192 + tg * 512 + 512],
                            start=(hl == 0), stop=(hl == 3))
                    return last
                P.op("pe", omm, ins=[bU1, bU2], outs=[bpq])
                sg, bsg = stg32[tg % 2], bstg[tg % 2]
                P.op("dve", lambda v, pq=pq, sg=sg: v.tensor_copy(sg[:], pq[:]),
                     ins=[bpq], outs=[bsg])
                P.op("sp", lambda s, sg=sg, dq=dq, sect=sect, lcol=lcol: s.dma_start(
                    out=rs_in[sect * D + dq * 128: sect * D + (dq + 1) * 128,
                              lcol:lcol + 512], in_=sg[:]),
                    ins=[bsg], outs=[brsin], dma=True)

        trs = P.op("pool", lambda g: g.collective_compute(
            "ReduceScatter", ALU.add,
            replica_groups=[[0, 1, 2, 3], [4, 5, 6, 7]],
            ins=[rs_in.ap().opt()], outs=[rs_out.ap().opt()]),
            waits=[P.tok("sp")], ins=[brsin], inc=1)
        if DBG_ATTN:
            y_rso = nc.dram_tensor("y_rso", [D, TPC], F32,
                                   kind="ExternalOutput").ap()
            P.op("sp", lambda s: s.dma_start(out=y_rso, in_=rs_out[:, :]),
                 waits=[trs], dma=True)
            y_rsi = nc.dram_tensor("y_rsi", [4 * D, TPC], F32,
                                   kind="ExternalOutput").ap()
            P.op("sp", lambda s: s.dma_start(out=y_rsi, in_=rs_in[:, :]),
                 ins=[brsin], dma=True)
        for dq in range(NDC):
            rb, brb = stg32[dq % 2], bstg[dq % 2]
            for cc4 in range(4):
                P.op("sp", lambda s, rb=rb, dq=dq, cc4=cc4: s.dma_start(
                    out=rb[:], in_=rs_out[dq * 128:(dq + 1) * 128,
                                          cc4 * 512:(cc4 + 1) * 512]),
                    waits=[trs], outs=[brb], dma=True)
                P.op("pool", lambda g, rb=rb, dq=dq, cc4=cc4: g.dma_start(
                    out=x_out[dq * 128:(dq + 1) * 128,
                              cc4 * 512:(cc4 + 1) * 512], in_=rb[:],
                    accum_op=ALU.add), ins=[brb], outs=[bxout], dma=True)
        for dq in range(NDC):
            P.op("dve", lambda v: v.memset(btile[:], 0.0), outs=[bbtile])
            P.op("dve", lambda v, dq=dq: v.tensor_scalar(
                out=btile[:], in0=btile[:], scalar1=bos[:, dq:dq + 1],
                scalar2=None, op0=ALU.add), ins=[bpar], outs=[bbtile])
            P.op("pool", lambda g, dq=dq: g.dma_start(
                out=x_out[dq * 128:(dq + 1) * 128, :], in_=btile[:],
                accum_op=ALU.add), ins=[bbtile], outs=[bxout], dma=True)

        P.raw_op("pool", lambda g: g.engine_nop(), waits=[P.tok("pool")])
        P.raw_op("sp", lambda s: s.nop(), waits=[P.tok("sp")])




_cache = {}
DBG_FUSED = False
PHASES = ("mamba", "moe", "attn", "l1ffn")
_EMITTERS = {"mamba": lambda *a: emit_mamba(*a), "moe": lambda *a: emit_moe(*a),
             "attn": lambda *a: emit_attn(*a), "l1ffn": lambda *a: emit_l1ffn(*a)}
_INPUT_SPECS = {"mamba": (MAMBA_INPUTS, {}), "moe": (MOE_INPUTS, {}),
                "attn": (ATTN_INPUTS, ATTN_INPUTS_I32),
                "l1ffn": (L1FFN_INPUTS, {})}


def _declare_phase_inputs(nc, phase):
    fspec, ispec = _INPUT_SPECS[phase]
    A = {}
    for nm, shp in fspec.items():
        A[nm] = nc.dram_tensor(nm, shp, F32, kind="ExternalInput").ap()
    for nm, shp in ispec.items():
        A[nm] = nc.dram_tensor(nm, shp, mybir.dt.int32, kind="ExternalInput").ap()
    return A


def build_single(phase):
    nc = bass.Bass("TRN2", target_bir_lowering=False)
    A = _declare_phase_inputs(nc, phase)
    A["x_in"] = nc.dram_tensor("x_in", [D, TPC], F32, kind="ExternalInput").ap()
    A["x_out"] = nc.dram_tensor("x_out", [D, TPC], F32, kind="ExternalOutput").ap()
    with ExitStack() as ctx:
        e = ctx.enter_context
        P = Prog(nc)
        cm = Common(P, nc, ctx)
        _EMITTERS[phase](P, nc, cm, A)
        P.barrier()
        blk = e(nc.Block())
        P.build(blk, ctx)
    return nc


def build_fused():
    nc = bass.Bass("TRN2", target_bir_lowering=False)
    A = {}
    for phase in PHASES:
        A.update(_declare_phase_inputs(nc, phase))
    x0 = nc.dram_tensor("x0", [D, TPC], F32, kind="ExternalInput").ap()
    xmid = [nc.dram_tensor(f"xmid{i}", [D, TPC], F32, kind="Internal").ap()
            for i in range(3)]
    x_out = nc.dram_tensor("x_out", [D, TPC], F32, kind="ExternalOutput").ap()
    chain = [x0, xmid[0], xmid[1], xmid[2], x_out]
    if DBG_FUSED:
        ymid = [nc.dram_tensor(f"ymid{i}", [D, TPC], F32, kind="ExternalOutput").ap()
                for i in range(3)]
    with ExitStack() as ctx:
        e = ctx.enter_context
        P = Prog(nc)
        cm = Common(P, nc, ctx)
        for i, phase in enumerate(PHASES):
            Ap = dict(A)
            Ap["x_in"] = chain[i]
            Ap["x_out"] = chain[i + 1]
            _EMITTERS[phase](P, nc, cm, Ap)
            P.barrier()
            if DBG_FUSED and i < 3:
                P.raw_op("sp", lambda s, i=i: s.dma_start(
                    out=ymid[i], in_=xmid[i]), dma=True)
                P.raw_op("sp", lambda s: s.nop(), waits=[P.tok("sp")])
        blk = e(nc.Block())
        P.build(blk, ctx)
    return nc


def get_prog(name):
    if name not in _cache:
        _cache[name] = build_fused() if name == "fused" else build_single(name)
    return _cache[name]


def prep_mamba(a):
    W_x = a["L0_attn_W_x"]
    WxT_bc = np.zeros((D, 64), np.float32)
    WxT_bc[:, 0:16] = W_x[128:144].T
    WxT_bc[:, 32:48] = W_x[144:160].T
    Apd = np.ascontiguousarray(
        a["L0_attn_A_log"].reshape(NDC, 128, 16).transpose(1, 0, 2)
        .reshape(128, NDC * 16))
    common = {
        "WxT_dt": np.ascontiguousarray(W_x[0:128].T),
        "WxT_bc": WxT_bc,
        "WdtT": np.ascontiguousarray(a["L0_attn_W_dt"].T),
        "bdt": vec_pd(a["L0_attn_b_dt"]), "Apd": Apd,
        "Dpd": vec_pd(a["L0_attn_D_param"]),
    }
    percore = []
    for c in range(NCORES):
        r = c % 4
        m0 = np.zeros((128, 8), np.float32)
        m1 = np.zeros((128, 8), np.float32)
        m0[:, r] = 1.0
        m1[:, 7 - r] = 1.0
        percore.append({"mask0": m0, "mask1": m1})
    return common, percore


def prep_moe(a):
    common = {
        "rwT": np.ascontiguousarray(a["L0_ffn_router_w"].T),
        "rbias": np.ascontiguousarray(
            np.broadcast_to(a["L0_ffn_router_bias"][None, :], (128, 16))),
        "Wu": a["L0_ffn_Wu"], "Wg": a["L0_ffn_Wg"], "Wd": a["L0_ffn_Wd"],
        "Su": a["L0_ffn_Su"][0], "Sg": a["L0_ffn_Sg"][0], "Sd": a["L0_ffn_Sd"][0],
    }
    return common, [{} for _ in range(NCORES)]


def prep_attn(a):
    W_qkv, b_qkv = a["L1_attn_W_qkv"], a["L1_attn_b_qkv"]
    W_out, b_out = a["L1_attn_W_out"], a["L1_attn_b_out"]
    Wr = W_qkv.reshape(D, 16, 3, 128)
    Wqkv_c = np.ascontiguousarray(
        np.concatenate([Wr[:, :, 0, :].reshape(D, D),
                        Wr[:, :, 1, :].reshape(D, D),
                        Wr[:, :, 2, :].reshape(D, D)], axis=1))
    br = b_qkv.reshape(16, 3, 128)
    common = {
        "Wqkv": Wqkv_c,
        "bq": vec_pd(np.ascontiguousarray(br[:, 0, :].reshape(D))),
        "bk": vec_pd(np.ascontiguousarray(br[:, 1, :].reshape(D))),
        "bv_row": np.ascontiguousarray(br[:, 2, :].reshape(1, D)),
        "bout": vec_pd(b_out),
    }
    percore = []
    for c in range(NCORES):
        r = c % 4
        p = np.arange(128)
        qidx = np.zeros((128, 32), np.int32)
        kidx = np.zeros((128, 32), np.int32)
        vidx = np.zeros((128, 64), np.int32)
        for hl in range(4):
            h = 4 * r + hl
            for qb in range(8):
                o, s = owner(qb), slot(qb)
                qidx[:, hl * 8 + qb] = (o * 6144 + h * 128 + p) * 16 + s * 8
            for j2 in range(8):
                o, s = owner(j2), slot(j2)
                kidx[:, hl * 8 + j2] = (o * 6144 + D + h * 128 + p) * 16 + s * 8
        for j2 in range(8):
            o, s = owner(j2), slot(j2)
            for aa in range(8):
                vidx[:, j2 * 8 + aa] = \
                    (o * 6144 + 2 * D + s * 1024 + aa * 128 + p) * 16 + 4 * r
        percore.append({
            "Wout": np.ascontiguousarray(W_out[r * 512:(r + 1) * 512, :]),
            "qidx": qidx, "kidx": kidx, "vidx": vidx})
    return common, percore


def prep_l1ffn(a):
    return {
        "wT": np.ascontiguousarray(a["L1_ffn_weight"].T),
        "gamma": vec_pd(a["L1_ffn_gamma"]), "beta": vec_pd(a["L1_ffn_beta"]),
        "lnw": vec_pd(a["L1_ffn_ln_weight"]), "lnb": vec_pd(a["L1_ffn_ln_bias"]),
    }, [{} for _ in range(NCORES)]


def _run_phase(phase, x_parts, common, percore):
    nc = get_prog(phase)
    in_maps = [{"x_in": x_parts[c], **common, **percore[c]}
               for c in range(NCORES)]
    res = run_bass_kernel_spmd(nc, in_maps, core_ids=list(range(NCORES)))
    return [res.results[c]["x_out"] for c in range(NCORES)]


def run_mamba(x_parts, W_x, W_dt, b_dt, A_log, D_param):
    c, p = prep_mamba({"L0_attn_W_x": W_x, "L0_attn_W_dt": W_dt,
                       "L0_attn_b_dt": b_dt, "L0_attn_A_log": A_log,
                       "L0_attn_D_param": D_param})
    return _run_phase("mamba", x_parts, c, p)


def run_moe(x_parts, router_w, router_bias, Wu, Wg, Wd, Su, Sg, Sd):
    c, p = prep_moe({"L0_ffn_router_w": router_w, "L0_ffn_router_bias": router_bias,
                     "L0_ffn_Wu": Wu, "L0_ffn_Wg": Wg, "L0_ffn_Wd": Wd,
                     "L0_ffn_Su": Su, "L0_ffn_Sg": Sg, "L0_ffn_Sd": Sd})
    return _run_phase("moe", x_parts, c, p)


def run_attn(x_parts, W_qkv, b_qkv, W_out, b_out):
    c, p = prep_attn({"L1_attn_W_qkv": W_qkv, "L1_attn_b_qkv": b_qkv,
                      "L1_attn_W_out": W_out, "L1_attn_b_out": b_out})
    return _run_phase("attn", x_parts, c, p)


def run_l1ffn(x_parts, weight, gamma, beta, ln_weight, ln_bias):
    c, p = prep_l1ffn({"L1_ffn_weight": weight, "L1_ffn_gamma": gamma,
                       "L1_ffn_beta": beta, "L1_ffn_ln_weight": ln_weight,
                       "L1_ffn_ln_bias": ln_bias})
    return _run_phase("l1ffn", x_parts, c, p)


def kernel(**inputs):
    a = {k: np.ascontiguousarray(np.asarray(v), dtype=np.float32)
         for k, v in inputs.items()}
    x_parts = shard_x(a["L0_attn_x"])
    cm_m, pc_m = prep_mamba(a)
    cm_o, pc_o = prep_moe(a)
    cm_a, pc_a = prep_attn(a)
    cm_f, pc_f = prep_l1ffn(a)
    nc = get_prog("fused")
    in_maps = []
    for c in range(NCORES):
        in_maps.append({"x0": x_parts[c], **cm_m, **pc_m[c], **cm_o, **pc_o[c],
                        **cm_a, **pc_a[c], **cm_f, **pc_f[c]})
    res = run_bass_kernel_spmd(nc, in_maps, core_ids=list(range(NCORES)))
    return unshard_x([res.results[c]["x_out"] for c in range(NCORES)])


# version 09 (final)
# speedup vs baseline: 1.5086x; 1.5086x over previous
import numpy as np
import sys

sys.path.insert(0, "/opt/trn_rl_repo")
import concourse.bass as bass
import concourse.mybir as mybir
from concourse.bass_utils import run_bass_kernel_spmd
from contextlib import ExitStack
import ml_dtypes

F32 = mybir.dt.float32
BF16 = mybir.dt.bfloat16
AF = mybir.ActivationFunctionType
ALU = mybir.AluOpType
AX = mybir.AxisListType

NCORES = 8
DBG_MAMBA = False
D = 2048
NDC = 16
TPC = 2048
BLK = 1024
EPS = 1e-5


def owner(j):
    return j if j < 4 else 7 - j


def slot(j):
    return 0 if j < 4 else 1


class Buf:
    __slots__ = ("w", "r")

    def __init__(self):
        self.w = None
        self.r = []


class Prog:
    ENGS = ("sp", "pe", "act", "dve", "pool", "sp2")

    def __init__(self, nc):
        self.nc = nc
        self.q = {e: [] for e in self.ENGS}
        self.cnt = {e: 0 for e in self.ENGS}
        self.auto = {"act", "dve", "pool"}
        self.semh = {}

    def raw_op(self, eng, fn, waits=(), dma=False, inc=None):
        amt = 16 if dma else (1 if inc is None else inc)
        red = {}
        for (we, wc) in waits:
            if wc and (we not in red or red[we] < wc):
                red[we] = wc
        if eng in self.auto and self.cnt[eng] > 0:
            red[eng] = max(red.get(eng, 0), self.cnt[eng])
        if eng == "pe":
            red.pop("pe", None)
        w = tuple((we, wc) for we, wc in red.items())
        self.cnt[eng] += amt
        self.q[eng].append((fn, w, amt))
        return (eng, self.cnt[eng])

    def op(self, eng, fn, ins=(), outs=(), waits=(), dma=False, inc=None):
        wl = list(waits)
        for b in ins:
            if b.w:
                wl.append(b.w)
        for b in outs:
            if b.w:
                wl.append(b.w)
            wl.extend(b.r)
        t = self.raw_op(eng, fn, waits=wl, dma=dma, inc=inc)
        for b in ins:
            b.r.append(t)
        for b in outs:
            b.w = t
            b.r = []
        return t

    def tok(self, eng):
        return (eng, self.cnt[eng])

    def barrier(self):
        snap = {e: self.cnt[e] for e in self.ENGS}
        for e in self.ENGS:
            if e == "sp2":
                continue
            waits = [(e2, c) for e2, c in snap.items() if c]
            self.raw_op(e, lambda r: r.nop(), waits=waits)

    def build(self, block, ctx):
        for e in self.ENGS:
            self.semh[e] = ctx.enter_context(self.nc.semaphore(f"sem_{e}"))
        for e in self.ENGS:
            queue = self.q[e]
            if not queue:
                continue
            semh = self.semh
            myself = self.semh[e]

            def replay(raw, queue=queue, myself=myself):
                for fn, waits, amt in queue:
                    for (we, wc) in waits:
                        raw.wait_ge(semh[we], wc)
                    inst = fn(raw)
                    inst.then_inc(myself, amt)

            if e == "sp":
                block.sync(replay)
            elif e == "sp2":
                raise AssertionError("sp2 queue unused (would collide with act)")
            elif e == "pe":
                block.tensor(replay)
            elif e == "act":
                block.scalar(replay)
            elif e == "dve":
                block.vector(replay)
            elif e == "pool":
                block.gpsimd(replay)


class Common:
    def __init__(self, P, nc, ctx):
        e = ctx.enter_context
        self.ones_col_bf = e(nc.sbuf_tensor("ones_col_bf", [128, 1], BF16))
        self.ones_row_f = e(nc.sbuf_tensor("ones_row_f", [1, 128], F32))
        self.ones65 = e(nc.sbuf_tensor("ones65", [65, 128], F32))
        self.ones_row_bf = e(nc.sbuf_tensor("ones_row_bf", [1, 128], BF16))
        self.ident_bf = e(nc.sbuf_tensor("ident_bf", [128, 128], BF16))
        self.ident_f = e(nc.sbuf_tensor("ident_f", [128, 128], F32))
        scr = self.ident_f
        P.raw_op("pool", lambda g: g.memset(self.ones_col_bf[:], 1.0))
        P.raw_op("pool", lambda g: g.memset(self.ones_row_f[:], 1.0))
        P.raw_op("pool", lambda g: g.memset(self.ones65[:], 1.0))
        P.raw_op("pool", lambda g: g.memset(self.ones_row_bf[:], 1.0))
        P.raw_op("pool", lambda g: g.iota(
            scr[:], pattern=[[1, 128]], base=0, channel_multiplier=-1,
            allow_small_or_imprecise_dtypes=True))
        t1 = P.raw_op("dve", lambda v: v.tensor_scalar(
            out=self.ident_bf[:], in0=scr[:], scalar1=0.0, scalar2=None,
            op0=ALU.is_equal), waits=[P.tok("pool")])
        t2 = P.raw_op("dve", lambda v: v.tensor_scalar(
            out=self.ident_f[:], in0=scr[:], scalar1=0.0, scalar2=None,
            op0=ALU.is_equal))
        self.ready = [P.tok("pool"), t1, t2]


class LnScratch:
    def __init__(self, nc, ctx, name):
        e = ctx.enter_context
        x0 = e(nc.sbuf_tensor(f"{name}_xsq0", [128, 512], BF16))
        self.xsq = [x0, x0]
        self._stats = e(nc.sbuf_tensor(f"{name}_st", [1, 3 * 512], F32))
        self.stat = [self._stats[0:1, i * 512:(i + 1) * 512] for i in range(3)]
        self.rstd_bc = e(nc.sbuf_tensor(f"{name}_rstd", [128, 512], BF16))
        self.mr_bc = e(nc.sbuf_tensor(f"{name}_mr", [128, 512], BF16))
        self.psA = e(nc.psum_tensor(f"{name}_psA", [1, 512], F32))
        self.psB = e(nc.psum_tensor(f"{name}_psB", [1, 512], F32))
        self.psbc = e(nc.psum_tensor(f"{name}_psbc", [128, 512], F32))
        self.b = {k: Buf() for k in
                  ("xsq0", "st0", "st1", "st2",
                   "rstd", "mr", "psA", "psB", "psbc")}
        self.b["xsq1"] = self.b["xsq0"]


def emit_ln(P, cm, X, bX, T, xln_out, bXout, ls,
            gamma_sb=None, beta_sb=None, bparams=None):
    b = ls.b
    pw = [bparams] if bparams else []
    for tc in range(T // 512):
        t0 = tc * 512
        for dc in range(NDC):
            P.op("pe", lambda pe, dc=dc, t0=t0: pe.matmul(
                ls.psA[:], cm.ones_col_bf[:],
                X[:, dc * T + t0: dc * T + t0 + 512],
                start=(dc == 0), stop=(dc == NDC - 1)),
                ins=[bX], outs=[b["psA"]], waits=cm.ready)
        for dc in range(NDC):
            sb = dc % 2
            P.op("act", lambda s, dc=dc, t0=t0, sb=sb: s.activation(
                out=ls.xsq[sb][:], in_=X[:, dc * T + t0: dc * T + t0 + 512],
                func=AF.Square), ins=[bX], outs=[b[f"xsq{sb}"]])
            P.op("pe", lambda pe, dc=dc, sb=sb: pe.matmul(
                ls.psB[:], cm.ones_col_bf[:], ls.xsq[sb][:],
                start=(dc == 0), stop=(dc == NDC - 1)),
                ins=[b[f"xsq{sb}"]], outs=[b["psB"]])
        P.op("dve", lambda v: v.tensor_scalar(
            out=ls.stat[0][:], in0=ls.psA[:], scalar1=1.0 / D, scalar2=None,
            op0=ALU.mult), ins=[b["psA"]], outs=[b["st0"]])
        P.op("dve", lambda v: v.scalar_tensor_tensor(
            out=ls.stat[1][:], in0=ls.stat[0][:], scalar=-1.0, in1=ls.stat[0][:],
            op0=ALU.mult, op1=ALU.mult), ins=[b["st0"]], outs=[b["st1"]])
        P.op("dve", lambda v: v.scalar_tensor_tensor(
            out=ls.stat[1][:], in0=ls.psB[:], scalar=1.0 / D, in1=ls.stat[1][:],
            op0=ALU.mult, op1=ALU.add), ins=[b["psB"], b["st1"]], outs=[b["st1"]])
        P.op("dve", lambda v: v.tensor_scalar(
            out=ls.stat[1][:], in0=ls.stat[1][:], scalar1=EPS, scalar2=None,
            op0=ALU.add), ins=[b["st1"]], outs=[b["st1"]])
        P.op("act", lambda s: s.activation(
            out=ls.stat[2][:], in_=ls.stat[1][:], func=AF.Sqrt),
            ins=[b["st1"]], outs=[b["st2"]])
        P.op("dve", lambda v: v.reciprocal(ls.stat[1][:], ls.stat[2][:]),
             ins=[b["st2"]], outs=[b["st1"]])
        P.op("dve", lambda v: v.tensor_mul(
            ls.stat[2][:], ls.stat[0][:], ls.stat[1][:]),
            ins=[b["st0"], b["st1"]], outs=[b["st2"]])
        P.op("pe", lambda pe: pe.matmul(
            ls.psbc[:], cm.ones_row_f[:], ls.stat[1][:], start=True, stop=True),
            ins=[b["st1"]], outs=[b["psbc"]])
        P.op("dve", lambda v: v.tensor_copy(ls.rstd_bc[:], ls.psbc[:]),
             ins=[b["psbc"]], outs=[b["rstd"]])
        P.op("pe", lambda pe: pe.matmul(
            ls.psbc[:], cm.ones_row_f[:], ls.stat[2][:], start=True, stop=True),
            ins=[b["st2"]], outs=[b["psbc"]])
        P.op("dve", lambda v: v.tensor_copy(ls.mr_bc[:], ls.psbc[:]),
             ins=[b["psbc"]], outs=[b["mr"]])
        for dc in range(NDC):
            xs = X[:, dc * T + t0: dc * T + t0 + 512]
            xo = xln_out[:, dc * T + t0: dc * T + t0 + 512]
            P.op("dve", lambda v, xs=xs, xo=xo: v.tensor_mul(
                xo, xs, ls.rstd_bc[:]),
                ins=[bX, b["rstd"]], outs=[bXout] if bXout is not bX else [bX])
            P.op("dve", lambda v, xo=xo: v.tensor_sub(xo, xo, ls.mr_bc[:]),
                 ins=[b["mr"]], outs=[bXout])
            if gamma_sb is not None:
                P.op("dve", lambda v, xo=xo, dc=dc: v.tensor_scalar(
                    out=xo, in0=xo, scalar1=gamma_sb[:, dc:dc + 1],
                    scalar2=beta_sb[:, dc:dc + 1], op0=ALU.mult, op1=ALU.add),
                    ins=pw, outs=[bXout])
    return bXout


def load_stream(P, nc, X, bX, x_in):
    return P.op("pool", lambda g: g.dma_start(
        out=X[:], in_=x_in.rearrange("(a p) t -> p a t", p=128)),
        outs=[bX], dma=True)


L1FFN_INPUTS = {"wT": [D, D], "gamma": [128, NDC], "beta": [128, NDC],
                "lnw": [128, NDC], "lnb": [128, NDC]}


def emit_l1ffn(P, nc, cm, A, pfx="lf"):
    x_in, x_out = A["x_in"], A["x_out"]
    wT, gamma, beta, lnw, lnb = A["wT"], A["gamma"], A["beta"], A["lnw"], A["lnb"]
    with ExitStack() as ctx:
        e = ctx.enter_context
        U = e(nc.sbuf_tensor(pfx + "U", [128, NDC * TPC], BF16))
        X = U
        wsg = U
        xn = e(nc.sbuf_tensor(pfx + "xn", [128, NDC * TPC], BF16))
        gam = e(nc.sbuf_tensor(pfx + "gam", [128, NDC], F32))
        bet = e(nc.sbuf_tensor(pfx + "bet", [128, NDC], F32))
        lw = e(nc.sbuf_tensor(pfx + "lw", [128, NDC], F32))
        lb = e(nc.sbuf_tensor(pfx + "lb", [128, NDC], F32))
        wf = [e(nc.sbuf_tensor(pfx + f"wf{i}", [128, 1024], F32)) for i in range(2)]
        absm = e(nc.sbuf_tensor(pfx + "absm", [128, 2], F32))
        maxrow = e(nc.sbuf_tensor(pfx + "maxrow", [1, 1], F32))
        sg = e(nc.sbuf_tensor(pfx + "sg", [128, NDC], F32))
        stage = [e(nc.sbuf_tensor(pfx + f"stage{i}", [128, 512], F32)) for i in range(2)]
        resd = [e(nc.sbuf_tensor(pfx + f"resd{i}", [128, 512], F32)) for i in range(2)]
        pstr = e(nc.psum_tensor(pfx + "pstr", [1, 128], F32))
        psout = [e(nc.psum_tensor(pfx + f"psout{i}", [128, 512], F32)) for i in range(2)]
        psb1 = e(nc.psum_tensor(pfx + "psb1", [128, 1], F32))
        ls = LnScratch(nc, ctx, pfx + "ln")

        bU = Buf()
        bX, bxn, bwsg, bparams = bU, Buf(), bU, Buf()
        bwf = [Buf(), Buf()]
        babsm, bmaxrow, bsg = Buf(), Buf(), Buf()
        bstage = [Buf(), Buf()]
        bresd = [Buf(), Buf()]
        bpsout = [Buf(), Buf()]
        bpstr, bpsb1 = Buf(), Buf()

        load_stream(P, nc, X, bX, x_in)
        for nm, dst in (("gamma", gam), ("beta", bet), ("lnw", lw), ("lnb", lb)):
            src = {"gamma": gamma, "beta": beta, "lnw": lnw, "lnb": lnb}[nm]
            P.op("sp", lambda s, dst=dst, src=src: s.dma_start(out=dst[:], in_=src),
                 outs=[bparams], dma=True)

        emit_ln(P, cm, X, bX, TPC, xn, bxn, ls)
        emit_ln(P, cm, xn, bxn, TPC, xn, bxn, ls,
                gamma_sb=lw, beta_sb=lb, bparams=bparams)

        hidx = 0
        for kc in range(NDC):
            for hf in range(2):
                f = hidx % 2
                P.op("sp", lambda s, kc=kc, hf=hf, f=f: s.dma_start(
                    out=wf[f][:],
                    in_=wT[kc * 128:(kc + 1) * 128, hf * 1024:(hf + 1) * 1024]),
                    outs=[bwf[f]], dma=True)
                P.op("act", lambda s, kc=kc, hf=hf, f=f: s.sign(
                    out=wsg[:, kc * D + hf * 1024: kc * D + (hf + 1) * 1024],
                    in_=wf[f][:]), ins=[bwf[f]], outs=[bwsg])
                col = absm[:, 0:1] if hidx == 0 else absm[:, 1:2]
                P.op("dve", lambda v, f=f, col=col: v.reduce_max(
                    col, wf[f][:], axis=AX.X, apply_absolute_value=True),
                    ins=[bwf[f]], outs=[babsm])
                if hidx > 0:
                    P.op("dve", lambda v: v.tensor_max(
                        absm[:, 0:1], absm[:, 0:1], absm[:, 1:2]),
                        ins=[babsm], outs=[babsm])
                hidx += 1

        P.op("pe", lambda pe: pe.transpose(pstr[:], absm[:, 0:1], cm.ident_f[:]),
             ins=[babsm], outs=[bpstr], waits=cm.ready)
        P.op("dve", lambda v: v.reduce_max(maxrow[:], pstr[:], axis=AX.X),
             ins=[bpstr], outs=[bmaxrow])
        P.op("pe", lambda pe: pe.matmul(
            psb1[:], cm.ones_row_f[:], maxrow[:], start=True, stop=True),
            ins=[bmaxrow], outs=[bpsb1])
        P.op("dve", lambda v: v.tensor_scalar(
            out=sg[:], in0=gam[:], scalar1=psb1[:], scalar2=None, op0=ALU.mult),
            ins=[bpsb1, bparams], outs=[bsg])

        si = 0
        for oc in range(NDC):
            for tcc in range(4):
                pb, bpb = psout[si % 2], bpsout[si % 2]
                stg, bstg = stage[si % 2], bstage[si % 2]
                for kc in range(NDC):
                    P.op("pe", lambda pe, oc=oc, tcc=tcc, kc=kc, pb=pb: pe.matmul(
                        pb[:], wsg[:, kc * D + oc * 128: kc * D + oc * 128 + 128],
                        xn[:, kc * TPC + tcc * 512: kc * TPC + tcc * 512 + 512],
                        start=(kc == 0), stop=(kc == NDC - 1)),
                        ins=[bwsg, bxn], outs=[bpb])
                P.op("dve", lambda v, pb=pb, stg=stg, oc=oc: v.tensor_scalar(
                    out=stg[:], in0=pb[:], scalar1=sg[:, oc:oc + 1],
                    scalar2=bet[:, oc:oc + 1], op0=ALU.mult, op1=ALU.add),
                    ins=[bpb, bsg, bparams], outs=[bstg])
                rb, brb = resd[si % 2], bresd[si % 2]
                P.op("sp", lambda s, rb=rb, oc=oc, tcc=tcc: s.dma_start(
                    out=rb[:],
                    in_=x_in[oc * 128:(oc + 1) * 128, tcc * 512:(tcc + 1) * 512]),
                    outs=[brb], dma=True)
                P.op("dve", lambda v, stg=stg, rb=rb: v.tensor_add(
                    stg[:], stg[:], rb[:]), ins=[brb], outs=[bstg])
                P.op("sp", lambda s, stg=stg, oc=oc, tcc=tcc: s.dma_start(
                    out=x_out[oc * 128:(oc + 1) * 128,
                              tcc * 512:(tcc + 1) * 512],
                    in_=stg[:]), ins=[bstg], dma=True)
                si += 1

        P.raw_op("sp", lambda s: s.nop(), waits=[P.tok("sp")])


def vec_pd(v):
    return np.ascontiguousarray(v.reshape(NDC, 128).T)


def shard_x(x):
    out = []
    for c in range(NCORES):
        b, r = c // 4, c % 4
        xc = np.concatenate([x[b, r * BLK:(r + 1) * BLK],
                             x[b, (7 - r) * BLK:(8 - r) * BLK]], axis=0)
        out.append(np.ascontiguousarray(xc.T))
    return out


def unshard_x(parts):
    x = np.empty((2, 8192, D), dtype=parts[0].dtype)
    for c in range(NCORES):
        b, r = c // 4, c % 4
        xc = parts[c].T
        x[b, r * BLK:(r + 1) * BLK] = xc[0:BLK]
        x[b, (7 - r) * BLK:(8 - r) * BLK] = xc[BLK:]
    return x


MOE_INPUTS = {"rwT": [D, 16], "rbias": [128, 16], "Wu": [16, D, 1024],
              "Wg": [16, D, 1024], "Wd": [16, 1024, D], "Su": [D, D],
              "Sg": [D, D], "Sd": [D, D]}


def emit_moe(P, nc, cm, A, pfx="mo"):
    x_in, x_out = A["x_in"], A["x_out"]
    rwT, rbias = A["rwT"], A["rbias"]
    Wu, Wg, Wd = A["Wu"], A["Wg"], A["Wd"]
    Su, Sg, Sd = A["Su"], A["Sg"], A["Sd"]
    with ExitStack() as ctx:
        e = ctx.enter_context
        U = e(nc.sbuf_tensor(pfx + "U", [128, NDC * TPC], BF16))
        X = U
        xln = e(nc.sbuf_tensor(pfx + "xln", [128, NDC * TPC], BF16))
        rw = e(nc.sbuf_tensor(pfx + "rw", [128, NDC * 16], BF16))
        rb = e(nc.sbuf_tensor(pfx + "rb", [128, 16], F32))
        iot16 = e(nc.sbuf_tensor(pfx + "iot16", [128, 16], F32))
        sel16 = e(nc.sbuf_tensor(pfx + "sel16", [16, 16 * 128], F32))
        lg = e(nc.sbuf_tensor(pfx + "lg", [128, 16], F32))
        zz = e(nc.sbuf_tensor(pfx + "zz", [128, 16], F32))
        eq = e(nc.sbuf_tensor(pfx + "eq", [128, 16], F32))
        tmp16 = e(nc.sbuf_tensor(pfx + "tmp16", [128, 16], F32))
        oh1 = e(nc.sbuf_tensor(pfx + "oh1", [128, 16], F32))
        oh2 = e(nc.sbuf_tensor(pfx + "oh2", [128, 16], F32))
        comb_tm = e(nc.sbuf_tensor(pfx + "comb_tm", [128, 16], F32))
        mcol = [e(nc.sbuf_tensor(pfx + f"mcol{i}", [128, 1], F32)) for i in range(6)]
        comb_fm = e(nc.sbuf_tensor(pfx + "comb_fm", [16, TPC], F32))
        comb_bc = e(nc.sbuf_tensor(pfx + "comb_bc", [128, TPC], BF16))
        wu_sb = [e(nc.sbuf_tensor(pfx + f"wu{i}", [128, NDC * 128], BF16)) for i in range(2)]
        wg_sb = [e(nc.sbuf_tensor(pfx + f"wg{i}", [128, NDC * 128], BF16)) for i in range(2)]
        hsc = e(nc.sbuf_tensor(pfx + "hsc", [128, 512], BF16))
        stage = [e(nc.sbuf_tensor(pfx + f"stg{i}", [128, TPC], F32)) for i in range(2)]
        ls = LnScratch(nc, ctx, pfx + "ln")
        ps_hu = e(nc.psum_tensor(pfx + "ps_hu", [128, 512], F32))
        ps_hg = e(nc.psum_tensor(pfx + "ps_hg", [128, 512], F32))
        ps_dn = [e(nc.psum_tensor(pfx + f"ps_dn{i}", [128, 512], F32)) for i in range(2)]

        bU, bxln, brw, brb = Buf(), Buf(), Buf(), Buf()
        biot, bsel = Buf(), Buf()
        blg, bzz, beq, btmp, boh1, boh2, bcomb_tm = (
            Buf(), Buf(), Buf(), Buf(), Buf(), Buf(), Buf())
        bm = [Buf() for _ in range(6)]
        bcomb_fm, bcomb_bc = Buf(), Buf()
        bwu = [Buf(), Buf()]
        bwg = [Buf(), Buf()]
        bhsc = Buf()
        bstage = [Buf(), Buf()]
        bps_hu, bps_hg = Buf(), Buf()
        bps_dn = [Buf(), Buf()]
        bpsbc = ls.b["psbc"]
        bh = [Buf() for _ in range(8)]
        bwd = [Buf() for _ in range(8)]
        bxout = [Buf() for _ in range(NDC)]

        load_stream(P, nc, X, bU, x_in)
        P.op("pool", lambda g: g.dma_start(
            out=rw[:], in_=rwT.rearrange("(a p) e -> p a e", p=128)),
            outs=[brw], dma=True)
        P.op("sp", lambda s: s.dma_start(out=rb[:], in_=rbias),
             outs=[brb], dma=True)
        P.op("pool", lambda g: g.iota(
            iot16[:], pattern=[[1, 16]], base=0, channel_multiplier=0,
            allow_small_or_imprecise_dtypes=True), outs=[biot])
        P.op("pool", lambda g: g.iota(
            sel16[:], pattern=[[1, 16], [0, 128]], base=0,
            channel_multiplier=-1, allow_small_or_imprecise_dtypes=True),
            outs=[bsel])
        P.op("dve", lambda v: v.tensor_scalar(
            out=sel16[:], in0=sel16[:], scalar1=0.0, scalar2=None,
            op0=ALU.is_equal), ins=[bsel], outs=[bsel])
        tinit = P.op("sp", lambda s: s.dma_start(out=x_out, in_=x_in), dma=True)
        for rbuf in bxout:
            rbuf.w = tinit

        emit_ln(P, cm, X, bU, TPC, xln, bxln, ls)
        for rbuf in bh + bwd:
            rbuf.r = [P.tok("pe"), P.tok("dve"), P.tok("act")]

        for tt in range(NDC):
            ps_r = ls.psbc[:, 0:16]
            def rmm(pe, tt=tt):
                last = None
                for dc in range(NDC):
                    last = pe.matmul(
                        ps_r, xln[:, dc * TPC + tt * 128: dc * TPC + tt * 128 + 128],
                        rw[:, dc * 16:(dc + 1) * 16],
                        start=(dc == 0), stop=(dc == NDC - 1))
                return last
            P.op("pe", rmm, ins=[bxln, brw], outs=[bpsbc], waits=cm.ready)
            P.op("dve", lambda v: v.tensor_copy(lg[:], ps_r), ins=[bpsbc], outs=[blg])
            P.op("dve", lambda v: v.tensor_add(zz[:], lg[:], rb[:]),
                 ins=[blg, brb], outs=[bzz])
            P.op("dve", lambda v: v.reduce_max(mcol[0][:], zz[:], axis=AX.X),
                 ins=[bzz], outs=[bm[0]])
            P.op("dve", lambda v: v.tensor_scalar(
                out=eq[:], in0=zz[:], scalar1=mcol[0][:], scalar2=None,
                op0=ALU.is_equal), ins=[bzz, bm[0]], outs=[beq])
            P.op("dve", lambda v: v.tensor_scalar(
                out=tmp16[:], in0=iot16[:], scalar1=1000.0, scalar2=None,
                op0=ALU.add), ins=[biot], outs=[btmp])
            P.op("dve", lambda v: v.scalar_tensor_tensor(
                out=tmp16[:], in0=eq[:], scalar=-1000.0, in1=tmp16[:],
                op0=ALU.mult, op1=ALU.add), ins=[beq], outs=[btmp])
            P.op("dve", lambda v: v.tensor_reduce(
                mcol[1][:], tmp16[:], axis=AX.X, op=ALU.min),
                ins=[btmp], outs=[bm[1]])
            P.op("dve", lambda v: v.tensor_scalar(
                out=oh1[:], in0=iot16[:], scalar1=mcol[1][:], scalar2=None,
                op0=ALU.is_equal), ins=[biot, bm[1]], outs=[boh1])
            P.op("dve", lambda v: v.tensor_mul(tmp16[:], lg[:], oh1[:]),
                 ins=[blg, boh1], outs=[btmp])
            P.op("dve", lambda v: v.reduce_sum(mcol[2][:], tmp16[:], axis=AX.X),
                 ins=[btmp], outs=[bm[2]])
            P.op("dve", lambda v: v.scalar_tensor_tensor(
                out=zz[:], in0=oh1[:], scalar=-30000.0, in1=zz[:],
                op0=ALU.mult, op1=ALU.add), ins=[boh1], outs=[bzz])
            P.op("dve", lambda v: v.reduce_max(mcol[0][:], zz[:], axis=AX.X),
                 ins=[bzz], outs=[bm[0]])
            P.op("dve", lambda v: v.tensor_scalar(
                out=eq[:], in0=zz[:], scalar1=mcol[0][:], scalar2=None,
                op0=ALU.is_equal), ins=[bzz, bm[0]], outs=[beq])
            P.op("dve", lambda v: v.tensor_scalar(
                out=tmp16[:], in0=iot16[:], scalar1=1000.0, scalar2=None,
                op0=ALU.add), ins=[biot], outs=[btmp])
            P.op("dve", lambda v: v.scalar_tensor_tensor(
                out=tmp16[:], in0=eq[:], scalar=-1000.0, in1=tmp16[:],
                op0=ALU.mult, op1=ALU.add), ins=[beq], outs=[btmp])
            P.op("dve", lambda v: v.tensor_reduce(
                mcol[1][:], tmp16[:], axis=AX.X, op=ALU.min),
                ins=[btmp], outs=[bm[1]])
            P.op("dve", lambda v: v.tensor_scalar(
                out=oh2[:], in0=iot16[:], scalar1=mcol[1][:], scalar2=None,
                op0=ALU.is_equal), ins=[biot, bm[1]], outs=[boh2])
            P.op("dve", lambda v: v.tensor_mul(tmp16[:], lg[:], oh2[:]),
                 ins=[blg, boh2], outs=[btmp])
            P.op("dve", lambda v: v.reduce_sum(mcol[3][:], tmp16[:], axis=AX.X),
                 ins=[btmp], outs=[bm[3]])
            P.op("dve", lambda v: v.tensor_max(mcol[4][:], mcol[2][:], mcol[3][:]),
                 ins=[bm[2], bm[3]], outs=[bm[4]])
            P.op("dve", lambda v: v.tensor_scalar(
                out=mcol[4][:], in0=mcol[4][:], scalar1=-1.0, scalar2=None,
                op0=ALU.mult), ins=[bm[4]], outs=[bm[4]])
            P.op("act", lambda s: s.activation(
                out=mcol[2][:], in_=mcol[2][:], func=AF.Exp, bias=mcol[4][:]),
                ins=[bm[2], bm[4]], outs=[bm[2]])
            P.op("act", lambda s: s.activation(
                out=mcol[3][:], in_=mcol[3][:], func=AF.Exp, bias=mcol[4][:]),
                ins=[bm[3], bm[4]], outs=[bm[3]])
            P.op("dve", lambda v: v.tensor_add(mcol[5][:], mcol[2][:], mcol[3][:]),
                 ins=[bm[2], bm[3]], outs=[bm[5]])
            P.op("dve", lambda v: v.reciprocal(mcol[5][:], mcol[5][:]),
                 ins=[bm[5]], outs=[bm[5]])
            P.op("dve", lambda v: v.tensor_mul(mcol[2][:], mcol[2][:], mcol[5][:]),
                 ins=[bm[2], bm[5]], outs=[bm[2]])
            P.op("dve", lambda v: v.tensor_mul(mcol[3][:], mcol[3][:], mcol[5][:]),
                 ins=[bm[3], bm[5]], outs=[bm[3]])
            P.op("dve", lambda v: v.tensor_scalar(
                out=comb_tm[:], in0=oh1[:], scalar1=mcol[2][:], scalar2=None,
                op0=ALU.mult), ins=[boh1, bm[2]], outs=[bcomb_tm])
            P.op("dve", lambda v: v.tensor_scalar(
                out=tmp16[:], in0=oh2[:], scalar1=mcol[3][:], scalar2=None,
                op0=ALU.mult), ins=[boh2, bm[3]], outs=[btmp])
            P.op("dve", lambda v: v.tensor_add(comb_tm[:], comb_tm[:], tmp16[:]),
                 ins=[btmp], outs=[bcomb_tm])
            P.op("pe", lambda pe: pe.transpose(
                ls.psbc[0:16, 0:128], comb_tm[:], cm.ident_f[:]),
                ins=[bcomb_tm], outs=[bpsbc], waits=cm.ready)
            P.op("dve", lambda v, tt=tt: v.tensor_copy(
                comb_fm[:, tt * 128:(tt + 1) * 128], ls.psbc[0:16, 0:128]),
                ins=[bpsbc], outs=[bcomb_fm])

        units = [(0, ee) for ee in range(16)] + [(1, 0), (1, 1)]
        for ui, (kind, idx) in enumerate(units):
            routed = (kind == 0)
            if routed:
                for tcc in range(4):
                    P.op("pe", lambda pe, idx=idx, tcc=tcc: pe.matmul(
                        ls.psbc[:], sel16[:, idx * 128:(idx + 1) * 128],
                        comb_fm[:, tcc * 512:(tcc + 1) * 512],
                        start=True, stop=True),
                        ins=[bsel, bcomb_fm], outs=[bpsbc])
                    P.op("dve", lambda v, tcc=tcc: v.tensor_copy(
                        comb_bc[:, tcc * 512:(tcc + 1) * 512], ls.psbc[:]),
                        ins=[bpsbc], outs=[bcomb_bc])
            for ec in range(8):
                f = ec % 2
                if routed:
                    usrc = Wu[idx, :, ec * 128:(ec + 1) * 128]
                    gsrc = Wg[idx, :, ec * 128:(ec + 1) * 128]
                    dsrc = Wd[idx, ec * 128:(ec + 1) * 128, :]
                else:
                    c0 = idx * 1024 + ec * 128
                    usrc = Su[:, c0:c0 + 128]
                    gsrc = Sg[:, c0:c0 + 128]
                    dsrc = Sd[c0:c0 + 128, :]
                P.op("pool", lambda g, usrc=usrc, f=f: g.dma_start(
                    out=wu_sb[f][:], in_=usrc.rearrange("(a p) c -> p a c", p=128)),
                    outs=[bwu[f]], dma=True)
                P.op("pool", lambda g, gsrc=gsrc, f=f: g.dma_start(
                    out=wg_sb[f][:], in_=gsrc.rearrange("(a p) c -> p a c", p=128)),
                    outs=[bwg[f]], dma=True)
                P.op("pool", lambda g, dsrc=dsrc, ec=ec: g.dma_start(
                    out=U[:, (8 + ec) * TPC:(9 + ec) * TPC], in_=dsrc),
                    outs=[bwd[ec]], dma=True)
                for tcc in range(4):
                    def umm(pe, f=f, tcc=tcc, w=wu_sb[f], ps=ps_hu):
                        last = None
                        for dc in range(NDC):
                            last = pe.matmul(
                                ps[:], w[:, dc * 128:(dc + 1) * 128],
                                xln[:, dc * TPC + tcc * 512: dc * TPC + tcc * 512 + 512],
                                start=(dc == 0), stop=(dc == NDC - 1))
                        return last
                    P.op("pe", umm, ins=[bwu[f], bxln], outs=[bps_hu])
                    def gmm(pe, f=f, tcc=tcc, w=wg_sb[f], ps=ps_hg):
                        last = None
                        for dc in range(NDC):
                            last = pe.matmul(
                                ps[:], w[:, dc * 128:(dc + 1) * 128],
                                xln[:, dc * TPC + tcc * 512: dc * TPC + tcc * 512 + 512],
                                start=(dc == 0), stop=(dc == NDC - 1))
                        return last
                    P.op("pe", gmm, ins=[bwg[f], bxln], outs=[bps_hg])
                    P.op("act", lambda s: s.activation(
                        out=hsc[:], in_=ps_hu[:], func=AF.Silu),
                        ins=[bps_hu], outs=[bhsc])
                    hdst = U[:, ec * TPC + tcc * 512: ec * TPC + tcc * 512 + 512]
                    if routed:
                        P.op("dve", lambda v, hdst=hdst: v.tensor_mul(
                            hdst, hsc[:], ps_hg[:]),
                            ins=[bhsc, bps_hg], outs=[bh[ec]])
                        P.op("dve", lambda v, hdst=hdst, tcc=tcc: v.tensor_mul(
                            hdst, hdst,
                            comb_bc[:, tcc * 512:(tcc + 1) * 512]),
                            ins=[bcomb_bc], outs=[bh[ec]])
                    else:
                        P.op("dve", lambda v, hdst=hdst: v.tensor_mul(
                            hdst, hsc[:], ps_hg[:]),
                            ins=[bhsc, bps_hg], outs=[bh[ec]])
            for dcout in range(NDC):
                sgb = stage[dcout % 2]
                bsgb = bstage[dcout % 2]
                for tcc in range(4):
                    pd = ps_dn[tcc % 2]
                    bpd = bps_dn[tcc % 2]
                    def dmm(pe, tcc=tcc, dcout=dcout, pd=pd):
                        last = None
                        for ec in range(8):
                            last = pe.matmul(
                                pd[:],
                                U[:, (8 + ec) * TPC + dcout * 128:
                                     (8 + ec) * TPC + dcout * 128 + 128],
                                U[:, ec * TPC + tcc * 512: ec * TPC + tcc * 512 + 512],
                                start=(ec == 0), stop=(ec == 7))
                        return last
                    P.op("pe", dmm, ins=bh + bwd, outs=[bpd])
                    P.op("dve", lambda v, sgb=sgb, tcc=tcc, pd=pd: v.tensor_copy(
                        sgb[:, tcc * 512:(tcc + 1) * 512], pd[:]),
                        ins=[bpd], outs=[bsgb])
                P.op("pool", lambda g, sgb=sgb, dcout=dcout: g.dma_start(
                    out=x_out[dcout * 128:(dcout + 1) * 128, :], in_=sgb[:],
                    accum_op=ALU.add), ins=[bsgb], outs=[bxout[dcout]], dma=True)

        P.raw_op("pool", lambda g: g.engine_nop(), waits=[P.tok("pool")])


MAMBA_INPUTS = {"WxT_dt": [D, 128], "WxT_bc": [D, 64], "WdtT": [128, D],
                "bdt": [128, NDC], "Apd": [128, NDC * 16], "Dpd": [128, NDC],
                "mask0": [128, 8], "mask1": [128, 8]}


def emit_mamba(P, nc, cm, A, pfx="mb"):
    x_in, x_out = A["x_in"], A["x_out"]
    WxT_dt, WxT_bc, WdtT = A["WxT_dt"], A["WxT_bc"], A["WdtT"]
    bdt, Apd, Dpd = A["bdt"], A["Apd"], A["Dpd"]
    mask0, mask1 = A["mask0"], A["mask1"]
    cc_in = nc.dram_tensor(pfx + "cc_in", [128, 1024], F32, kind="Internal")
    cc_out = nc.dram_tensor(pfx + "cc_out", [512, 1024], F32, kind="Internal")

    GL = 1024
    NG = 2

    with ExitStack() as ctx:
        e = ctx.enter_context
        U = e(nc.sbuf_tensor(pfx + "U", [128, NDC * TPC], BF16))
        X = U
        xln = e(nc.sbuf_tensor(pfx + "xln", [128, NDC * TPC], BF16))
        ov = e(nc.sbuf_tensor(pfx + "ov", [128, 3072], BF16))
        wxd = ov[:, 0:2048]
        wxbc = ov[:, 2048:3072]
        wdt = e(nc.sbuf_tensor(pfx + "wdt", [128, NDC * 128], BF16))
        bdtp = e(nc.sbuf_tensor(pfx + "bdtp", [128, NDC], F32))
        dpd_sb = e(nc.sbuf_tensor(pfx + "dpd_sb", [128, NDC], F32))
        aneg = e(nc.sbuf_tensor(pfx + "aneg", [128, NDC * 16], F32))
        selb = e(nc.sbuf_tensor(pfx + "selb", [64, 16 * 128], BF16))
        selc = e(nc.sbuf_tensor(pfx + "selc", [64, 16 * 128], BF16))
        bbc = U[:, 0:16 * GL]
        cbc = U[:, 16 * GL:32 * GL]
        bc4 = e(nc.sbuf_tensor(pfx + "bc4", [64, NG * GL], BF16))
        dtraw = e(nc.sbuf_tensor(pfx + "dtraw", [128, GL], BF16))
        zt = e(nc.sbuf_tensor(pfx + "zt", [128, GL], F32))
        dt_sb = e(nc.sbuf_tensor(pfx + "dt_sb", [128, GL], BF16))
        S_sb = zt
        dtx = e(nc.sbuf_tensor(pfx + "dtx", [128, GL], BF16))
        da = [e(nc.sbuf_tensor(pfx + f"da{i}", [128, GL], BF16)) for i in range(2)]
        dbx = [e(nc.sbuf_tensor(pfx + f"dbx{i}", [128, GL], BF16)) for i in range(2)]
        h_sb = [e(nc.sbuf_tensor(pfx + f"h_sb{i}", [128, GL], BF16)) for i in range(2)]
        _tmp0 = e(nc.sbuf_tensor(pfx + "tmp0", [128, GL], BF16))
        tmp = [_tmp0, _tmp0]
        y_acc = e(nc.sbuf_tensor(pfx + "y_acc", [128, GL], BF16))
        stageA = [e(nc.sbuf_tensor(pfx + "stA0", [128, GL], F32))] * 2
        t3 = stageA[0]
        hp = stageA[0][:, 0:512]
        S_tot = e(nc.sbuf_tensor(pfx + "S_tot", [128, NG * NDC], F32))
        Hc = e(nc.sbuf_tensor(pfx + "Hc", [128, 256], F32))
        summH = e(nc.sbuf_tensor(pfx + "summH", [128, 512], F32))
        summP = e(nc.sbuf_tensor(pfx + "summP", [128, 512], F32))
        h_run = Hc
        hi_all = e(nc.sbuf_tensor(pfx + "hi_all", [128, 8 * 256], F32))
        hi_t = e(nc.sbuf_tensor(pfx + "hi", [128, 256], F32))
        hi = [hi_t, hi_t]
        msk = [e(nc.sbuf_tensor(pfx + f"msk{i}", [128, 8], F32)) for i in range(2)]
        ls = LnScratch(nc, ctx, pfx + "ln")
        ps_a = [e(nc.psum_tensor(pfx + f"ps_a{i}", [128, 512], F32)) for i in range(2)]
        ps_b = [e(nc.psum_tensor(pfx + f"ps_b{i}", [128, 512], F32)) for i in range(2)]

        bU, bxln = Buf(), Buf()
        bpar = Buf()
        bsel = Buf()
        bbc4, bdtraw4 = Buf(), Buf()
        bbbc, bcbc = Buf(), Buf()
        bzt, bdt_sb, bdtx, byacc = (Buf() for _ in range(4))
        bS = bzt
        bda = [Buf(), Buf()]
        bdbx = [Buf(), Buf()]
        bh = [Buf(), Buf()]
        _btmp0 = Buf()
        btmp = [_btmp0, _btmp0]
        _bstA0 = Buf()
        bstA = [_bstA0, _bstA0]
        bt3 = _bstA0
        bhp = _bstA0
        bStot, bHc, bsumm, bhiall = (Buf() for _ in range(4))
        bhrun = bHc
        _bhi = Buf()
        bhi = [_bhi, _bhi]
        bmsk = [Buf(), Buf()]
        bps_a = [Buf(), Buf()]
        bps_b = [Buf(), Buf()]
        bxout = [Buf() for _ in range(NG * 16)]

        load_stream(P, nc, X, bU, x_in)
        P.op("pool", lambda g: g.dma_start(
            out=wxd[:], in_=WxT_dt.rearrange("(a p) c -> p a c", p=128)),
            outs=[bpar], dma=True)
        P.op("pool", lambda g: g.dma_start(
            out=wxbc[:], in_=WxT_bc.rearrange("(a p) c -> p a c", p=128)),
            outs=[bpar], dma=True)
        P.op("pool", lambda g: g.dma_start(
            out=wdt[:], in_=WdtT), outs=[bpar], dma=True)
        for nm, dst in (("bdt", bdtp), ("Apd", aneg), ("Dpd", dpd_sb),
                        ("mask0", msk[0]), ("mask1", msk[1])):
            srcs = {"bdt": bdt, "Apd": Apd, "Dpd": Dpd,
                    "mask0": mask0, "mask1": mask1}
            P.op("sp", lambda s, dst=dst, src=srcs[nm]: s.dma_start(
                out=dst[:], in_=src), outs=[bpar], dma=True)
        P.op("act", lambda s: s.activation(out=aneg[:], in_=aneg[:], func=AF.Exp),
             ins=[bpar], outs=[bpar])
        P.op("dve", lambda v: v.tensor_scalar(
            out=aneg[:], in0=aneg[:], scalar1=-1.0, scalar2=None, op0=ALU.mult),
            ins=[bpar], outs=[bpar])
        P.op("pool", lambda g: g.iota(
            selb[:], pattern=[[1, 16], [0, 128]], base=0, channel_multiplier=-1,
            allow_small_or_imprecise_dtypes=True), outs=[bsel])
        P.op("dve", lambda v: v.tensor_scalar(
            out=selb[:], in0=selb[:], scalar1=0.0, scalar2=None, op0=ALU.is_equal),
            ins=[bsel], outs=[bsel])
        P.op("pool", lambda g: g.iota(
            selc[:], pattern=[[1, 16], [0, 128]], base=32, channel_multiplier=-1,
            allow_small_or_imprecise_dtypes=True), outs=[bsel])
        P.op("dve", lambda v: v.tensor_scalar(
            out=selc[:], in0=selc[:], scalar1=0.0, scalar2=None, op0=ALU.is_equal),
            ins=[bsel], outs=[bsel])
        P.op("dve", lambda v: v.memset(Hc[:], 0.0), outs=[bHc])
        tinit = P.op("sp", lambda s: s.dma_start(out=x_out, in_=x_in), dma=True)
        for rb in bxout:
            rb.w = tinit

        emit_ln(P, cm, X, bU, TPC, xln, bxln, ls)
        for rb in (bbbc, bcbc):
            rb.r = [P.tok("pe"), P.tok("dve"), P.tok("act")]

        for g in range(NG):
            for hf in range(2):
                t0 = g * GL + hf * 512
                pb, bpb = ps_b[hf], bps_b[hf]
                def mm_bc(pe, t0=t0, pb=pb):
                    last = None
                    for dc in range(NDC):
                        last = pe.matmul(
                            pb[0:64, :], wxbc[:, dc * 64:(dc + 1) * 64],
                            xln[:, dc * TPC + t0: dc * TPC + t0 + 512],
                            start=(dc == 0), stop=(dc == NDC - 1))
                    return last
                P.op("pe", mm_bc, ins=[bpar, bxln], outs=[bpb], waits=cm.ready)
                P.op("dve", lambda v, g=g, hf=hf, pb=pb: v.tensor_copy(
                    bc4[0:64, g * GL + hf * 512: g * GL + (hf + 1) * 512],
                    pb[0:64, :]), ins=[bpb], outs=[bbc4])

        def xz_dtraw(g):
            for hf in range(2):
                t0 = g * GL + hf * 512
                pa, bpa = ps_a[hf], bps_a[hf]
                def mm_dt(pe, t0=t0, pa=pa):
                    last = None
                    for dc in range(NDC):
                        last = pe.matmul(
                            pa[:], wxd[:, dc * 128:(dc + 1) * 128],
                            xln[:, dc * TPC + t0: dc * TPC + t0 + 512],
                            start=(dc == 0), stop=(dc == NDC - 1))
                    return last
                P.op("pe", mm_dt, ins=[bpar, bxln], outs=[bpa], waits=cm.ready)
                P.op("dve", lambda v, hf=hf, pa=pa: v.tensor_copy(
                    dtraw[:, hf * 512:(hf + 1) * 512], pa[:]),
                    ins=[bpa], outs=[bdtraw4])

        def group_pre(g, dc, save_S=False, want_dtx=False):
            t0 = g * GL
            for hf in range(2):
                pa, bpa = ps_a[hf], bps_a[hf]
                P.op("pe", lambda pe, dc=dc, hf=hf, pa=pa: pe.matmul(
                    pa[:], wdt[:, dc * 128:(dc + 1) * 128],
                    dtraw[:, hf * 512:(hf + 1) * 512],
                    start=True, stop=True),
                    ins=[bpar, bdtraw4], outs=[bpa])
                P.op("dve", lambda v, dc=dc, hf=hf, pa=pa: v.tensor_scalar(
                    out=zt[:, hf * 512:(hf + 1) * 512], in0=pa[:],
                    scalar1=bdtp[:, dc:dc + 1], scalar2=None, op0=ALU.add),
                    ins=[bpa, bpar], outs=[bzt])
            P.op("act", lambda s: s.activation(out=t3[:], in_=zt[:], func=AF.Abs),
                 ins=[bzt], outs=[bt3])
            P.op("act", lambda s: s.activation(
                out=t3[:], in_=t3[:], func=AF.Exp, scale=-1.0),
                ins=[bt3], outs=[bt3])
            P.op("act", lambda s: s.activation(
                out=t3[:], in_=t3[:], func=AF.Ln, bias=1.0),
                ins=[bt3], outs=[bt3])
            P.op("dve", lambda v: v.tensor_scalar(
                out=zt[:], in0=zt[:], scalar1=0.0, scalar2=None, op0=ALU.max),
                ins=[bzt], outs=[bzt])
            P.op("dve", lambda v: v.tensor_add(dt_sb[:], zt[:], t3[:]),
                 ins=[bzt, bt3], outs=[bdt_sb])
            P.op("dve", lambda v: v.tensor_tensor_scan(
                out=S_sb[:], data0=dt_sb[:], data1=dt_sb[:], initial=0.0,
                op0=ALU.add, op1=ALU.bypass),
                ins=[bdt_sb], outs=[bS])
            if save_S:
                P.op("dve", lambda v, g=g, dc=dc: v.tensor_copy(
                    S_tot[:, g * 16 + dc: g * 16 + dc + 1], S_sb[:, GL - 1:GL]),
                    ins=[bS], outs=[bStot])
            if want_dtx:
                P.op("dve", lambda v, dc=dc, t0=t0: v.tensor_mul(
                    dtx[:], dt_sb[:],
                    xln[:, dc * TPC + t0: dc * TPC + t0 + GL]),
                    ins=[bdt_sb, bxln], outs=[bdtx])

        def bcast_bc(g, want_b=True):
            for n in range(16):
                for hf in range(2):
                    pa, bpa = ps_a[hf], bps_a[hf]
                    pb, bpb = ps_b[hf], bps_b[hf]
                    if want_b:
                        P.op("pe", lambda pe, n=n, g=g, hf=hf, pa=pa: pe.matmul(
                            pa[:], selb[:, n * 128:(n + 1) * 128],
                            bc4[0:64, g * GL + hf * 512: g * GL + (hf + 1) * 512],
                            start=True, stop=True),
                            ins=[bsel, bbc4], outs=[bpa])
                        P.op("dve", lambda v, n=n, hf=hf, pa=pa: v.tensor_copy(
                            bbc[:, n * GL + hf * 512: n * GL + (hf + 1) * 512],
                            pa[:]), ins=[bpa], outs=[bbbc])
                    P.op("pe", lambda pe, n=n, g=g, hf=hf, pb=pb: pe.matmul(
                        pb[:], selc[:, n * 128:(n + 1) * 128],
                        bc4[0:64, g * GL + hf * 512: g * GL + (hf + 1) * 512],
                        start=True, stop=True),
                        ins=[bsel, bbc4], outs=[bpb])
                    P.op("dve", lambda v, n=n, hf=hf, pb=pb: v.tensor_copy(
                        cbc[:, n * GL + hf * 512: n * GL + (hf + 1) * 512],
                        pb[:]), ins=[bpb], outs=[bcbc])

        for g in range(NG):
            sl = g
            t0 = g * GL
            xz_dtraw(g)
            bcast_bc(g, want_b=True)
            for dc in range(NDC):
                group_pre(g, dc, save_S=True, want_dtx=True)
                P.op("dve", lambda v: v.memset(y_acc[:], 0.0), outs=[byacc])
                for n in range(16):
                    f = n % 2
                    acol = aneg[:, dc * 16 + n: dc * 16 + n + 1]
                    P.op("act", lambda s, acol=acol, f=f: s.activation(
                        out=da[f][:], in_=dt_sb[:], func=AF.Exp, scale=acol),
                        ins=[bdt_sb, bpar], outs=[bda[f]])
                    P.op("pool", lambda g2, n=n, f=f: g2.tensor_mul(
                        dbx[f][:], dtx[:], bbc[:, n * GL:(n + 1) * GL]),
                        ins=[bdtx, bbbc], outs=[bdbx[f]])
                    P.op("dve", lambda v, f=f: v.tensor_tensor_scan(
                        out=h_sb[f][:], data0=da[f][:], data1=dbx[f][:],
                        initial=0.0, op0=ALU.mult, op1=ALU.add),
                        ins=[bda[f], bdbx[f]], outs=[bh[f]])
                    hcol = Hc[:, dc * 16 + n: dc * 16 + n + 1]
                    P.op("dve", lambda v, hcol=hcol, f=f: v.tensor_copy(
                        hcol, h_sb[f][:, GL - 1:GL]), ins=[bh[f]], outs=[bHc])
                    P.op("dve", lambda v, n=n, f=f: v.tensor_mul(
                        tmp[f][:], h_sb[f][:], cbc[:, n * GL:(n + 1) * GL]),
                        ins=[bh[f], bcbc], outs=[btmp[f]])
                    P.op("dve", lambda v, f=f: v.tensor_add(
                        y_acc[:], y_acc[:], tmp[f][:]),
                        ins=[btmp[f]], outs=[byacc])
                st, bst = stageA[dc % 2], bstA[dc % 2]
                P.op("dve", lambda v, st=st, dc=dc, t0=t0: v.scalar_tensor_tensor(
                    out=st[:], in0=xln[:, dc * TPC + t0: dc * TPC + t0 + GL],
                    scalar=dpd_sb[:, dc:dc + 1], in1=y_acc[:],
                    op0=ALU.mult, op1=ALU.add),
                    ins=[bxln, bpar, byacc], outs=[bst])
                P.op("pool", lambda g2, st=st, dc=dc, t0=t0: g2.dma_start(
                    out=x_out[dc * 128:(dc + 1) * 128, t0:t0 + GL], in_=st[:],
                    accum_op=ALU.add),
                    ins=[bst], outs=[bxout[g * 16 + dc]], dma=True)
                tot = S_tot[:, sl * 16 + dc: sl * 16 + dc + 1]
                P.op("dve", lambda v, dc=dc, tot=tot, sl=sl: v.tensor_scalar(
                    out=summP[:, sl * 256 + dc * 16: sl * 256 + (dc + 1) * 16],
                    in0=aneg[:, dc * 16:(dc + 1) * 16], scalar1=tot,
                    scalar2=None, op0=ALU.mult),
                    ins=[bpar, bStot], outs=[bsumm])
            P.op("dve", lambda v, sl=sl: v.tensor_copy(
                summH[:, sl * 256:(sl + 1) * 256], Hc[:]),
                ins=[bHc], outs=[bsumm])
        P.op("act", lambda s: s.activation(
            out=summP[:], in_=summP[:], func=AF.Exp),
            ins=[bsumm], outs=[bsumm])

        tst1 = P.op("pool", lambda g: g.dma_start(
            out=cc_in[:, 0:512], in_=summH[:]), ins=[bsumm], dma=True)
        tst2 = P.op("pool", lambda g: g.dma_start(
            out=cc_in[:, 512:1024], in_=summP[:]), ins=[bsumm], dma=True)
        tcc = P.op("pool", lambda g: g.collective_compute(
            "AllGather", ALU.bypass,
            replica_groups=[[0, 1, 2, 3], [4, 5, 6, 7]],
            ins=[cc_in.ap().opt()], outs=[cc_out.ap().opt()]),
            waits=[tst1, tst2], inc=1)
        P.op("dve", lambda v: v.memset(h_run[:], 0.0), outs=[bhrun])
        order = [(0, 0), (1, 0), (2, 0), (3, 0), (3, 1), (2, 1), (1, 1), (0, 1)]
        for m, (rr, ss) in enumerate(order):
            P.op("sp", lambda s, rr=rr, ss=ss: s.dma_start(
                out=hp[:, 0:256],
                in_=cc_out[rr * 128:(rr + 1) * 128, ss * 256:(ss + 1) * 256]),
                waits=[tcc], outs=[bhp], dma=True)
            P.op("sp", lambda s, rr=rr, ss=ss: s.dma_start(
                out=hp[:, 256:512],
                in_=cc_out[rr * 128:(rr + 1) * 128,
                           512 + ss * 256: 512 + (ss + 1) * 256]),
                waits=[tcc], outs=[bhp], dma=True)
            P.op("dve", lambda v, m=m: v.tensor_copy(
                hi_all[:, m * 256:(m + 1) * 256], h_run[:]),
                ins=[bhrun], outs=[bhiall])
            P.op("dve", lambda v: v.tensor_mul(
                h_run[:], h_run[:], hp[:, 256:512]),
                ins=[bhp], outs=[bhrun])
            P.op("dve", lambda v: v.tensor_add(
                h_run[:], h_run[:], hp[:, 0:256]),
                ins=[bhp], outs=[bhrun])
        for g in range(NG):
            sl = g
            t0 = g * GL
            P.op("dve", lambda v, sl=sl: v.memset(hi[sl][:], 0.0), outs=[bhi[sl]])
            for m in range(8):
                P.op("dve", lambda v, sl=sl, m=m: v.scalar_tensor_tensor(
                    out=hi[sl][:], in0=hi_all[:, m * 256:(m + 1) * 256],
                    scalar=msk[sl][:, m:m + 1], in1=hi[sl][:],
                    op0=ALU.mult, op1=ALU.add),
                    ins=[bhiall, bmsk[sl]], outs=[bhi[sl]])
            xz_dtraw(g)
            bcast_bc(g, want_b=False)
            for dc in range(NDC):
                group_pre(g, dc, save_S=False, want_dtx=False)
                P.op("dve", lambda v: v.memset(y_acc[:], 0.0), outs=[byacc])
                for n in range(16):
                    f = n % 2
                    acol = aneg[:, dc * 16 + n: dc * 16 + n + 1]
                    hcol = hi[sl][:, dc * 16 + n: dc * 16 + n + 1]
                    P.op("act", lambda s, acol=acol, f=f: s.activation(
                        out=da[f][:], in_=S_sb[:], func=AF.Exp, scale=acol),
                        ins=[bS, bpar], outs=[bda[f]])
                    P.op("dve", lambda v, hcol=hcol, f=f: v.tensor_scalar(
                        out=tmp[f][:], in0=da[f][:], scalar1=hcol, scalar2=None,
                        op0=ALU.mult), ins=[bda[f], bhi[sl]], outs=[btmp[f]])
                    P.op("dve", lambda v, n=n, f=f: v.tensor_mul(
                        tmp[f][:], tmp[f][:], cbc[:, n * GL:(n + 1) * GL]),
                        ins=[bcbc], outs=[btmp[f]])
                    P.op("dve", lambda v, f=f: v.tensor_add(
                        y_acc[:], y_acc[:], tmp[f][:]),
                        ins=[btmp[f]], outs=[byacc])
                st, bst = stageA[dc % 2], bstA[dc % 2]
                P.op("dve", lambda v, st=st: v.tensor_copy(st[:], y_acc[:]),
                     ins=[byacc], outs=[bst])
                P.op("pool", lambda g2, st=st, dc=dc, t0=t0: g2.dma_start(
                    out=x_out[dc * 128:(dc + 1) * 128, t0:t0 + GL], in_=st[:],
                    accum_op=ALU.add),
                    ins=[bst], outs=[bxout[g * 16 + dc]], dma=True)

        P.raw_op("pool", lambda g: g.engine_nop(), waits=[P.tok("pool")])
        P.raw_op("sp", lambda s: s.nop(), waits=[P.tok("sp")])

ATTN_LVL = 7
DBG_ATTN = False


ATTN_INPUTS = {"Wqkv": [D, 3 * D], "bq": [128, NDC], "bk": [128, NDC],
               "bv_row": [1, D], "Wout": [512, D], "bout": [128, NDC]}
ATTN_INPUTS_I32 = {"qidx": [128, 32], "kidx": [128, 32], "vidx": [128, 64]}


def emit_attn(P, nc, cm, A, pfx="at"):
    x_in, x_out = A["x_in"], A["x_out"]
    Wqkv, bq, bk, bv_row = A["Wqkv"], A["bq"], A["bk"], A["bv_row"]
    Wout, bout = A["Wout"], A["bout"]
    qidx, kidx, vidx = A["qidx"], A["kidx"], A["vidx"]
    ag_in = nc.dram_tensor(pfx + "ag_in", [3 * D, TPC], BF16, kind="Internal")
    ag_out = nc.dram_tensor(pfx + "ag_out", [4 * 3 * D, TPC], BF16, kind="Internal")
    rs_in = nc.dram_tensor(pfx + "rs_in", [4 * D, TPC], F32, kind="Internal")
    rs_out = nc.dram_tensor(pfx + "rs_out", [D, TPC], F32, kind="Internal")
    ag_view = ag_out.ap().rearrange("r (a c) -> (r a) c", c=128)

    ISQ = 0.08838834764831845
    with ExitStack() as ctx:
        e = ctx.enter_context
        U1 = e(nc.sbuf_tensor(pfx + "U1", [128, NDC * TPC], BF16))
        xln = U1
        U2 = e(nc.sbuf_tensor(pfx + "U2", [128, NDC * TPC], BF16))
        X = U2
        wq_s = [U2[:, 0:2048], U2[:, 2048:4096]]
        wv_s = U2[:, 4096:12288]
        bvbc = U2[:, 12288:14336]
        qkstg = [U2[:, 14336 + i * 512:14336 + (i + 1) * 512] for i in range(2)]
        V_sb = U2[:, 0:32768]
        wout_sb = U2[:, 0:8192]
        K_sb = e(nc.sbuf_tensor(pfx + "K_sb", [128, 8192], BF16))
        Q_sb = e(nc.sbuf_tensor(pfx + "Q_sb", [128, 1024], BF16))
        P_sb = [e(nc.sbuf_tensor(pfx + f"P_sb{i}", [128, 1024], BF16)) for i in range(2)]
        rl_row = e(nc.sbuf_tensor(pfx + "rl_row", [1, 1024], F32))
        rl_bc = e(nc.sbuf_tensor(pfx + "rl_bc", [128, 1024], BF16))
        m01 = e(nc.sbuf_tensor(pfx + "m01", [128, 2048], BF16))
        m01f = e(nc.sbuf_tensor(pfx + "m01f", [128, 2048], F32))
        idxs = e(nc.sbuf_tensor(pfx + "idxs", [128, 128], mybir.dt.int32))
        bqs = e(nc.sbuf_tensor(pfx + "bqs", [128, NDC], F32))
        bks = e(nc.sbuf_tensor(pfx + "bks", [128, NDC], F32))
        bvs = e(nc.sbuf_tensor(pfx + "bvs", [1, D], F32))
        bos = e(nc.sbuf_tensor(pfx + "bos", [128, NDC], F32))
        stg32 = [e(nc.sbuf_tensor(pfx + f"stg32_{i}", [128, 512], F32)) for i in range(2)]
        btile = e(nc.sbuf_tensor(pfx + "btile", [128, TPC], F32))
        ls = LnScratch(nc, ctx, pfx + "ln")
        ps_q = [e(nc.psum_tensor(pfx + f"ps_q{i}", [128, 512], F32)) for i in range(2)]
        ps_s = [e(nc.psum_tensor(pfx + f"ps_s{i}", [128, 512], F32)) for i in range(2)]
        ps_ctx = ps_q
        ps_l = [ls.psA, ls.psB]

        bU1, bU2 = Buf(), Buf()
        bpar, bidx = Buf(), Buf()
        bK, bQ = Buf(), Buf()
        bP = [Buf(), Buf()]
        brl_row, brl_bc, bm01 = Buf(), Buf(), Buf()
        bstg = [Buf(), Buf()]
        bps_q = [Buf(), Buf()]
        bps_s = [Buf(), Buf()]
        bps_ctx = bps_q
        bps_l = [ls.b["psA"], ls.b["psB"]]
        bbtile = Buf()
        bagin, bagout, brsin = Buf(), Buf(), Buf()
        bxout = Buf()

        load_stream(P, nc, X, bU2, x_in)
        for nm, dst in (("bq", bqs), ("bk", bks), ("bv", bvs), ("bo", bos),
                        ("ix", idxs)):
            src = {"bq": bq, "bk": bk, "bv": bv_row, "bo": bout, "ix": None}[nm]
            if nm == "ix":
                continue
            P.op("sp", lambda s, dst=dst, src=src: s.dma_start(out=dst[:], in_=src),
                 outs=[bpar], dma=True)
        P.op("sp", lambda s: s.dma_start(out=idxs[:, 0:32], in_=qidx),
             outs=[bidx], dma=True)
        P.op("sp", lambda s: s.dma_start(out=idxs[:, 32:64], in_=kidx),
             outs=[bidx], dma=True)
        P.op("sp", lambda s: s.dma_start(out=idxs[:, 64:128], in_=vidx),
             outs=[bidx], dma=True)
        P.op("pool", lambda g: g.iota(
            m01f[:], pattern=[[1, 2048]], base=-1024, channel_multiplier=-1,
            allow_small_or_imprecise_dtypes=True), outs=[bm01])
        P.op("dve", lambda v: v.tensor_scalar(
            out=m01[:], in0=m01f[:], scalar1=0.0, scalar2=None, op0=ALU.is_ge),
            ins=[bm01], outs=[bm01])

        emit_ln(P, cm, X, bU2, TPC, xln, bU1, ls)
        bU2.r.extend([P.tok("pe"), P.tok("dve"), P.tok("act")])

        for part, bias_sb, scale in ((0, bqs, ISQ), (1, bks, None)):
            for jt in range(NDC):
                f = jt % 2
                wsrc = Wqkv[:, part * D + jt * 128: part * D + (jt + 1) * 128]
                P.op("pool", lambda g, wsrc=wsrc, f=f: g.dma_start(
                    out=wq_s[f], in_=wsrc.rearrange("(a p) c -> p a c", p=128)),
                    outs=[bU2], dma=True)
                for tcc in range(4):
                    pq, bpq = ps_q[tcc % 2], bps_q[tcc % 2]
                    def qmm(pe, f=f, tcc=tcc, pq=pq):
                        last = None
                        for dc in range(NDC):
                            last = pe.matmul(
                                pq[:], wq_s[f][:, dc * 128:(dc + 1) * 128],
                                xln[:, dc * TPC + tcc * 512: dc * TPC + tcc * 512 + 512],
                                start=(dc == 0), stop=(dc == NDC - 1))
                        return last
                    P.op("pe", qmm, ins=[bU2, bU1], outs=[bpq], waits=cm.ready)
                    stg = qkstg[tcc % 2]
                    if scale is not None:
                        P.op("dve", lambda v, pq=pq, stg=stg, jt=jt, bias_sb=bias_sb:
                             v.tensor_scalar(
                                 out=stg, in0=pq[:], scalar1=bias_sb[:, jt:jt + 1],
                                 scalar2=ISQ, op0=ALU.add, op1=ALU.mult),
                             ins=[bpq, bpar], outs=[bU2])
                    else:
                        P.op("dve", lambda v, pq=pq, stg=stg, jt=jt, bias_sb=bias_sb:
                             v.tensor_scalar(
                                 out=stg, in0=pq[:], scalar1=bias_sb[:, jt:jt + 1],
                                 scalar2=None, op0=ALU.add),
                             ins=[bpq, bpar], outs=[bU2])
                    P.op("sp", lambda s, stg=stg, part=part, jt=jt, tcc=tcc:
                         s.dma_start(
                             out=ag_in[part * D + jt * 128:
                                       part * D + (jt + 1) * 128,
                                       tcc * 512:(tcc + 1) * 512], in_=stg),
                         ins=[bU2], outs=[bagin], dma=True)
        for vc in range(4):
            P.op("pe", lambda pe, vc=vc: pe.matmul(
                ls.psbc[:], cm.ones_row_f[:], bvs[0:1, vc * 512:(vc + 1) * 512],
                start=True, stop=True),
                ins=[bpar], outs=[ls.b["psbc"]], waits=cm.ready)
            P.op("dve", lambda v, vc=vc: v.tensor_copy(
                bvbc[:, vc * 512:(vc + 1) * 512]
                if False else U2[:, 12288 + vc * 512:12288 + (vc + 1) * 512],
                ls.psbc[:]), ins=[ls.b["psbc"]], outs=[bU2])
        for vc in range(4):
            wsrc = Wqkv[:, 2 * D + vc * 512: 2 * D + (vc + 1) * 512]
            P.op("pool", lambda g, wsrc=wsrc: g.dma_start(
                out=wv_s, in_=wsrc.rearrange("(a p) c -> p a c", p=128)),
                outs=[bU2], dma=True)
            for tt in range(NDC):
                pq, bpq = ps_q[tt % 2], bps_q[tt % 2]
                def vmm(pe, vc=vc, tt=tt, pq=pq):
                    last = None
                    for dc in range(NDC):
                        last = pe.matmul(
                            pq[:], xln[:, dc * TPC + tt * 128: dc * TPC + tt * 128 + 128],
                            wv_s[:, dc * 512:(dc + 1) * 512],
                            start=(dc == 0), stop=(dc == NDC - 1))
                    return last
                P.op("pe", vmm, ins=[bU2, bU1], outs=[bpq])
                stg = qkstg[tt % 2]
                P.op("dve", lambda v, pq=pq, stg=stg, vc=vc: v.tensor_add(
                    stg, pq[:], U2[:, 12288 + vc * 512:12288 + (vc + 1) * 512]),
                    ins=[bpq], outs=[bU2])
                P.op("sp", lambda s, stg=stg, vc=vc, tt=tt: s.dma_start(
                    out=ag_in[2 * D + tt * 128: 2 * D + (tt + 1) * 128,
                              vc * 512:(vc + 1) * 512], in_=stg),
                    ins=[bU2], outs=[bagin], dma=True)

        tag = P.op("pool", lambda g: g.collective_compute(
            "AllGather", ALU.bypass,
            replica_groups=[[0, 1, 2, 3], [4, 5, 6, 7]],
            ins=[ag_in.ap().opt()], outs=[ag_out.ap().opt()]),
            waits=[P.tok("sp")], ins=[bagin], inc=1)
        bagout.w = tag


        vload = [None] * 64
        for j2 in range(8):
            for aa in range(8):
                col = j2 * 8 + aa
                t = P.op("pool", lambda g, col=col: g.indirect_dma_start(
                    out=V_sb[:, col * 512:(col + 1) * 512], out_offset=None,
                    in_=ag_view,
                    in_offset=bass.IndirectOffsetOnAxis(idxs[:, 64 + col:65 + col], 0)),
                    ins=[bidx], waits=[tag], outs=[bU2], dma=True)
                vload[col] = t

        for hl in range(4):
            for qb in range(8):
                P.op("pool", lambda g, hl=hl, qb=qb: g.indirect_dma_start(
                    out=Q_sb[:], out_offset=None, in_=ag_view,
                    in_offset=bass.IndirectOffsetOnAxis(
                        idxs[:, hl * 8 + qb: hl * 8 + qb + 1], 0)),
                    ins=[bidx], waits=[tag], outs=[bQ], dma=True)
                if hl == 0 or True:
                    pass
                if qb == 0:
                    krange = [0]
                else:
                    krange = [qb]
                for j2 in krange:
                    P.op("pool", lambda g, hl=hl, j2=j2: g.indirect_dma_start(
                        out=K_sb[:, j2 * 1024:(j2 + 1) * 1024], out_offset=None,
                        in_=ag_view,
                        in_offset=bass.IndirectOffsetOnAxis(
                            idxs[:, 32 + hl * 8 + j2: 32 + hl * 8 + j2 + 1], 0)),
                        ins=[bidx], waits=[tag], outs=[bK], dma=True)
                nkb = (qb + 1) * 8

                def stage1(kb):
                    psb, bpsb = P_sb[kb % 2], bP[kb % 2]
                    for half in range(2):
                        pss, bpss = ps_s[half], bps_s[half]
                        P.op("pe", lambda pe, kb=kb, half=half, pss=pss: pe.matmul(
                            pss[:], K_sb[:, kb * 128:(kb + 1) * 128],
                            Q_sb[:, half * 512:(half + 1) * 512],
                            start=True, stop=True),
                            ins=[bK, bQ], outs=[bpss])
                        P.op("act", lambda s, pss=pss, psb=psb, half=half:
                             s.activation(
                                 out=psb[:, half * 512:(half + 1) * 512],
                                 in_=pss[:], func=AF.Exp),
                             ins=[bpss], outs=[bpsb])

                def stage2(kb):
                    psb, bpsb = P_sb[kb % 2], bP[kb % 2]
                    if kb >= qb * 8:
                        off = 1024 - 128 * (kb - qb * 8)
                        P.op("dve", lambda v, psb=psb, off=off: v.tensor_mul(
                            psb[:], psb[:], m01[:, off:off + 1024]),
                            ins=[bm01], outs=[bpsb])
                    for half in range(2):
                        P.op("pe", lambda pe, kb=kb, half=half, psb=psb: pe.matmul(
                            ps_l[half][:], cm.ones_col_bf[:],
                            psb[:, half * 512:(half + 1) * 512],
                            start=(kb == 0), stop=(kb == nkb - 1)),
                            ins=[bpsb], outs=[bps_l[half]])
                        P.op("pe", lambda pe, kb=kb, half=half, psb=psb, hl=hl:
                             pe.matmul(
                                 ps_ctx[half][:], V_sb[:, kb * 512 + hl * 128:
                                                       kb * 512 + hl * 128 + 128],
                                 psb[:, half * 512:(half + 1) * 512],
                                 start=(kb == 0), stop=(kb == nkb - 1)),
                             ins=[bpsb, bU2], outs=[bps_ctx[half]])

                stage1(0)
                for kb in range(1, nkb):
                    stage1(kb)
                    stage2(kb - 1)
                stage2(nkb - 1)
                for half in range(2):
                    P.op("dve", lambda v, half=half: v.tensor_copy(
                        rl_row[0:1, half * 512:(half + 1) * 512], ps_l[half][:]),
                        ins=[bps_l[half]], outs=[brl_row])
                P.op("dve", lambda v: v.reciprocal(rl_row[:], rl_row[:]),
                     ins=[brl_row], outs=[brl_row])
                for half in range(2):
                    P.op("pe", lambda pe, half=half: pe.matmul(
                        ls.psbc[:], cm.ones_row_f[:],
                        rl_row[0:1, half * 512:(half + 1) * 512],
                        start=True, stop=True),
                        ins=[brl_row], outs=[ls.b["psbc"]])
                    P.op("dve", lambda v, half=half: v.tensor_copy(
                        rl_bc[:, half * 512:(half + 1) * 512], ls.psbc[:]),
                        ins=[ls.b["psbc"]], outs=[brl_bc])
                for half in range(2):
                    dst = U1[:, hl * 8192 + qb * 1024 + half * 512:
                             hl * 8192 + qb * 1024 + (half + 1) * 512]
                    P.op("dve", lambda v, dst=dst, half=half: v.tensor_mul(
                        dst, ps_ctx[half][:],
                        rl_bc[:, half * 512:(half + 1) * 512]),
                        ins=[bps_ctx[half], brl_bc], outs=[bU1])

        if DBG_ATTN:
            y_ctx = nc.dram_tensor("y_ctx", [512, 8192], BF16,
                                   kind="ExternalOutput").ap()
            for hl in range(4):
                P.op("sp", lambda s, hl=hl: s.dma_start(
                    out=y_ctx[hl * 128:(hl + 1) * 128, :],
                    in_=U1[:, hl * 8192:(hl + 1) * 8192]),
                    ins=[bU1], dma=True)
        P.op("pool", lambda g: g.dma_start(
            out=wout_sb, in_=Wout.rearrange("(a p) c -> p a c", p=128)),
            outs=[bU2], dma=True)
        tinit = P.op("sp", lambda s: s.dma_start(out=x_out, in_=x_in), dma=True)
        bxout.w = tinit
        for dq in range(NDC):
            for tg in range(16):
                qb = tg // 2
                sect = owner(qb)
                lcol = slot(qb) * 1024 + (tg % 2) * 512
                pq, bpq = ps_q[tg % 2], bps_q[tg % 2]
                def omm(pe, dq=dq, tg=tg, pq=pq):
                    last = None
                    for hl in range(4):
                        last = pe.matmul(
                            pq[:],
                            U2[:, hl * 2048 + dq * 128:
                                  hl * 2048 + dq * 128 + 128],
                            U1[:, hl * 8192 + tg * 512: hl * 8192 + tg * 512 + 512],
                            start=(hl == 0), stop=(hl == 3))
                    return last
                P.op("pe", omm, ins=[bU1, bU2], outs=[bpq])
                sg, bsg = stg32[tg % 2], bstg[tg % 2]
                P.op("dve", lambda v, pq=pq, sg=sg: v.tensor_copy(sg[:], pq[:]),
                     ins=[bpq], outs=[bsg])
                P.op("sp", lambda s, sg=sg, dq=dq, sect=sect, lcol=lcol: s.dma_start(
                    out=rs_in[sect * D + dq * 128: sect * D + (dq + 1) * 128,
                              lcol:lcol + 512], in_=sg[:]),
                    ins=[bsg], outs=[brsin], dma=True)

        trs = P.op("pool", lambda g: g.collective_compute(
            "ReduceScatter", ALU.add,
            replica_groups=[[0, 1, 2, 3], [4, 5, 6, 7]],
            ins=[rs_in.ap().opt()], outs=[rs_out.ap().opt()]),
            waits=[P.tok("sp")], ins=[brsin], inc=1)
        if DBG_ATTN:
            y_rso = nc.dram_tensor("y_rso", [D, TPC], F32,
                                   kind="ExternalOutput").ap()
            P.op("sp", lambda s: s.dma_start(out=y_rso, in_=rs_out[:, :]),
                 waits=[trs], dma=True)
            y_rsi = nc.dram_tensor("y_rsi", [4 * D, TPC], F32,
                                   kind="ExternalOutput").ap()
            P.op("sp", lambda s: s.dma_start(out=y_rsi, in_=rs_in[:, :]),
                 ins=[brsin], dma=True)
        for dq in range(NDC):
            rb, brb = stg32[dq % 2], bstg[dq % 2]
            for cc4 in range(4):
                P.op("sp", lambda s, rb=rb, dq=dq, cc4=cc4: s.dma_start(
                    out=rb[:], in_=rs_out[dq * 128:(dq + 1) * 128,
                                          cc4 * 512:(cc4 + 1) * 512]),
                    waits=[trs], outs=[brb], dma=True)
                P.op("pool", lambda g, rb=rb, dq=dq, cc4=cc4: g.dma_start(
                    out=x_out[dq * 128:(dq + 1) * 128,
                              cc4 * 512:(cc4 + 1) * 512], in_=rb[:],
                    accum_op=ALU.add), ins=[brb], outs=[bxout], dma=True)
        for dq in range(NDC):
            P.op("dve", lambda v: v.memset(btile[:], 0.0), outs=[bbtile])
            P.op("dve", lambda v, dq=dq: v.tensor_scalar(
                out=btile[:], in0=btile[:], scalar1=bos[:, dq:dq + 1],
                scalar2=None, op0=ALU.add), ins=[bpar], outs=[bbtile])
            P.op("pool", lambda g, dq=dq: g.dma_start(
                out=x_out[dq * 128:(dq + 1) * 128, :], in_=btile[:],
                accum_op=ALU.add), ins=[bbtile], outs=[bxout], dma=True)

        P.raw_op("pool", lambda g: g.engine_nop(), waits=[P.tok("pool")])
        P.raw_op("sp", lambda s: s.nop(), waits=[P.tok("sp")])




_cache = {}
DBG_FUSED = False
PHASES = ("mamba", "moe", "attn", "l1ffn")
_EMITTERS = {"mamba": lambda *a: emit_mamba(*a), "moe": lambda *a: emit_moe(*a),
             "attn": lambda *a: emit_attn(*a), "l1ffn": lambda *a: emit_l1ffn(*a)}
_INPUT_SPECS = {"mamba": (MAMBA_INPUTS, {}), "moe": (MOE_INPUTS, {}),
                "attn": (ATTN_INPUTS, ATTN_INPUTS_I32),
                "l1ffn": (L1FFN_INPUTS, {})}


def _declare_phase_inputs(nc, phase):
    fspec, ispec = _INPUT_SPECS[phase]
    A = {}
    for nm, shp in fspec.items():
        A[nm] = nc.dram_tensor(nm, shp, F32, kind="ExternalInput").ap()
    for nm, shp in ispec.items():
        A[nm] = nc.dram_tensor(nm, shp, mybir.dt.int32, kind="ExternalInput").ap()
    return A


def build_single(phase):
    nc = bass.Bass("TRN2", target_bir_lowering=False)
    A = _declare_phase_inputs(nc, phase)
    A["x_in"] = nc.dram_tensor("x_in", [D, TPC], F32, kind="ExternalInput").ap()
    A["x_out"] = nc.dram_tensor("x_out", [D, TPC], F32, kind="ExternalOutput").ap()
    with ExitStack() as ctx:
        e = ctx.enter_context
        P = Prog(nc)
        cm = Common(P, nc, ctx)
        _EMITTERS[phase](P, nc, cm, A)
        P.barrier()
        blk = e(nc.Block())
        P.build(blk, ctx)
    return nc


def build_fused():
    nc = bass.Bass("TRN2", target_bir_lowering=False)
    A = {}
    for phase in PHASES:
        A.update(_declare_phase_inputs(nc, phase))
    x0 = nc.dram_tensor("x0", [D, TPC], F32, kind="ExternalInput").ap()
    xmid = [nc.dram_tensor(f"xmid{i}", [D, TPC], F32, kind="Internal").ap()
            for i in range(3)]
    x_out = nc.dram_tensor("x_out", [D, TPC], F32, kind="ExternalOutput").ap()
    chain = [x0, xmid[0], xmid[1], xmid[2], x_out]
    if DBG_FUSED:
        ymid = [nc.dram_tensor(f"ymid{i}", [D, TPC], F32, kind="ExternalOutput").ap()
                for i in range(3)]
    with ExitStack() as ctx:
        e = ctx.enter_context
        P = Prog(nc)
        cm = Common(P, nc, ctx)
        for i, phase in enumerate(PHASES):
            Ap = dict(A)
            Ap["x_in"] = chain[i]
            Ap["x_out"] = chain[i + 1]
            _EMITTERS[phase](P, nc, cm, Ap)
            P.barrier()
            if DBG_FUSED and i < 3:
                P.raw_op("sp", lambda s, i=i: s.dma_start(
                    out=ymid[i], in_=xmid[i]), dma=True)
                P.raw_op("sp", lambda s: s.nop(), waits=[P.tok("sp")])
        blk = e(nc.Block())
        P.build(blk, ctx)
    return nc


def get_prog(name):
    if name not in _cache:
        _cache[name] = build_fused() if name == "fused" else build_single(name)
    return _cache[name]


def prep_mamba(a):
    W_x = a["L0_attn_W_x"]
    WxT_bc = np.zeros((D, 64), np.float32)
    WxT_bc[:, 0:16] = W_x[128:144].T
    WxT_bc[:, 32:48] = W_x[144:160].T
    Apd = np.ascontiguousarray(
        a["L0_attn_A_log"].reshape(NDC, 128, 16).transpose(1, 0, 2)
        .reshape(128, NDC * 16))
    common = {
        "WxT_dt": np.ascontiguousarray(W_x[0:128].T),
        "WxT_bc": WxT_bc,
        "WdtT": np.ascontiguousarray(a["L0_attn_W_dt"].T),
        "bdt": vec_pd(a["L0_attn_b_dt"]), "Apd": Apd,
        "Dpd": vec_pd(a["L0_attn_D_param"]),
    }
    percore = []
    for c in range(NCORES):
        r = c % 4
        m0 = np.zeros((128, 8), np.float32)
        m1 = np.zeros((128, 8), np.float32)
        m0[:, r] = 1.0
        m1[:, 7 - r] = 1.0
        percore.append({"mask0": m0, "mask1": m1})
    return common, percore


def prep_moe(a):
    common = {
        "rwT": np.ascontiguousarray(a["L0_ffn_router_w"].T),
        "rbias": np.ascontiguousarray(
            np.broadcast_to(a["L0_ffn_router_bias"][None, :], (128, 16))),
        "Wu": a["L0_ffn_Wu"], "Wg": a["L0_ffn_Wg"], "Wd": a["L0_ffn_Wd"],
        "Su": a["L0_ffn_Su"][0], "Sg": a["L0_ffn_Sg"][0], "Sd": a["L0_ffn_Sd"][0],
    }
    return common, [{} for _ in range(NCORES)]


def prep_attn(a):
    W_qkv, b_qkv = a["L1_attn_W_qkv"], a["L1_attn_b_qkv"]
    W_out, b_out = a["L1_attn_W_out"], a["L1_attn_b_out"]
    Wr = W_qkv.reshape(D, 16, 3, 128)
    Wqkv_c = np.ascontiguousarray(
        np.concatenate([Wr[:, :, 0, :].reshape(D, D),
                        Wr[:, :, 1, :].reshape(D, D),
                        Wr[:, :, 2, :].reshape(D, D)], axis=1))
    br = b_qkv.reshape(16, 3, 128)
    common = {
        "Wqkv": Wqkv_c,
        "bq": vec_pd(np.ascontiguousarray(br[:, 0, :].reshape(D))),
        "bk": vec_pd(np.ascontiguousarray(br[:, 1, :].reshape(D))),
        "bv_row": np.ascontiguousarray(br[:, 2, :].reshape(1, D)),
        "bout": vec_pd(b_out),
    }
    percore = []
    for c in range(NCORES):
        r = c % 4
        p = np.arange(128)
        qidx = np.zeros((128, 32), np.int32)
        kidx = np.zeros((128, 32), np.int32)
        vidx = np.zeros((128, 64), np.int32)
        for hl in range(4):
            h = 4 * r + hl
            for qb in range(8):
                o, s = owner(qb), slot(qb)
                qidx[:, hl * 8 + qb] = (o * 6144 + h * 128 + p) * 16 + s * 8
            for j2 in range(8):
                o, s = owner(j2), slot(j2)
                kidx[:, hl * 8 + j2] = (o * 6144 + D + h * 128 + p) * 16 + s * 8
        for j2 in range(8):
            o, s = owner(j2), slot(j2)
            for aa in range(8):
                vidx[:, j2 * 8 + aa] = \
                    (o * 6144 + 2 * D + s * 1024 + aa * 128 + p) * 16 + 4 * r
        percore.append({
            "Wout": np.ascontiguousarray(W_out[r * 512:(r + 1) * 512, :]),
            "qidx": qidx, "kidx": kidx, "vidx": vidx})
    return common, percore


def prep_l1ffn(a):
    return {
        "wT": np.ascontiguousarray(a["L1_ffn_weight"].T),
        "gamma": vec_pd(a["L1_ffn_gamma"]), "beta": vec_pd(a["L1_ffn_beta"]),
        "lnw": vec_pd(a["L1_ffn_ln_weight"]), "lnb": vec_pd(a["L1_ffn_ln_bias"]),
    }, [{} for _ in range(NCORES)]


def _run_phase(phase, x_parts, common, percore):
    nc = get_prog(phase)
    in_maps = [{"x_in": x_parts[c], **common, **percore[c]}
               for c in range(NCORES)]
    res = run_bass_kernel_spmd(nc, in_maps, core_ids=list(range(NCORES)))
    return [res.results[c]["x_out"] for c in range(NCORES)]


def run_mamba(x_parts, W_x, W_dt, b_dt, A_log, D_param):
    c, p = prep_mamba({"L0_attn_W_x": W_x, "L0_attn_W_dt": W_dt,
                       "L0_attn_b_dt": b_dt, "L0_attn_A_log": A_log,
                       "L0_attn_D_param": D_param})
    return _run_phase("mamba", x_parts, c, p)


def run_moe(x_parts, router_w, router_bias, Wu, Wg, Wd, Su, Sg, Sd):
    c, p = prep_moe({"L0_ffn_router_w": router_w, "L0_ffn_router_bias": router_bias,
                     "L0_ffn_Wu": Wu, "L0_ffn_Wg": Wg, "L0_ffn_Wd": Wd,
                     "L0_ffn_Su": Su, "L0_ffn_Sg": Sg, "L0_ffn_Sd": Sd})
    return _run_phase("moe", x_parts, c, p)


def run_attn(x_parts, W_qkv, b_qkv, W_out, b_out):
    c, p = prep_attn({"L1_attn_W_qkv": W_qkv, "L1_attn_b_qkv": b_qkv,
                      "L1_attn_W_out": W_out, "L1_attn_b_out": b_out})
    return _run_phase("attn", x_parts, c, p)


def run_l1ffn(x_parts, weight, gamma, beta, ln_weight, ln_bias):
    c, p = prep_l1ffn({"L1_ffn_weight": weight, "L1_ffn_gamma": gamma,
                       "L1_ffn_beta": beta, "L1_ffn_ln_weight": ln_weight,
                       "L1_ffn_ln_bias": ln_bias})
    return _run_phase("l1ffn", x_parts, c, p)


def kernel(**inputs):
    a = {k: np.ascontiguousarray(np.asarray(v), dtype=np.float32)
         for k, v in inputs.items()}
    x_parts = shard_x(a["L0_attn_x"])
    cm_m, pc_m = prep_mamba(a)
    cm_o, pc_o = prep_moe(a)
    cm_a, pc_a = prep_attn(a)
    cm_f, pc_f = prep_l1ffn(a)
    nc = get_prog("fused")
    in_maps = []
    for c in range(NCORES):
        in_maps.append({"x0": x_parts[c], **cm_m, **pc_m[c], **cm_o, **pc_o[c],
                        **cm_a, **pc_a[c], **cm_f, **pc_f[c]})
    res = run_bass_kernel_spmd(nc, in_maps, core_ids=list(range(NCORES)))
    return unshard_x([res.results[c]["x_out"] for c in range(NCORES)])
